# version 02
# baseline (speedup 1.0000x reference)
import math
from contextlib import ExitStack

import numpy as np
import ml_dtypes
import concourse.bass as bass
import concourse.mybir as mybir
from concourse.bass_utils import run_bass_kernel_spmd
from concourse.ap import AP

F32 = mybir.dt.float32
BF16 = mybir.dt.bfloat16
AF = mybir.ActivationFunctionType
ALU = mybir.AluOpType
AX = mybir.AxisListType
BF = ml_dtypes.bfloat16

QUEUES = ("sp", "te", "dve", "act", "pool")
ENG_ATTR = {"sp": "sync", "te": "tensor", "dve": "vector", "act": "scalar", "pool": "gpsimd"}


N_DMA_SEMS = {"sp": 16, "pool": 16}


class Prog:

    def __init__(self, nc):
        self.nc = nc
        self.st = ExitStack()
        self.items = {q: [] for q in QUEUES}
        self.sem_total = {}
        self.sem_order = []
        for q in QUEUES:
            self._new_sem(f"c_{q}")
        self.dma_sems = {}
        for q, n in N_DMA_SEMS.items():
            self.dma_sems[q] = [f"d_{q}{i}" for i in range(n)]
            for s in self.dma_sems[q]:
                self._new_sem(s)
        self.dma_rr = {q: 0 for q in N_DMA_SEMS}
        self.dma_last_ev = {}
        self.clocks = {q: {} for q in QUEUES}
        self.snaps = {}
        self.writes = {}
        self.readers = {}
        self.qlast = {q: None for q in QUEUES}
        self.sems = {}
        self._sb_bytes = 0

    def _new_sem(self, name):
        self.sem_total[name] = 0
        self.sem_order.append(name)

    def sb(self, name, shape, dtype=F32):
        t = self.st.enter_context(self.nc.sbuf_tensor(name, list(shape), dtype))
        esz = 2 if dtype in (BF16, mybir.dt.float16) else 4
        self._sb_bytes += shape[0] * int(np.prod(shape[1:])) * esz
        return t

    def ps(self, name, shape, dtype=F32):
        return self.st.enter_context(self.nc.psum_tensor(name, list(shape), dtype))

    def _wait(self, q, ev):
        sem, val = ev
        clk = self.clocks[q]
        if clk.get(sem, 0) >= val:
            return
        self.items[q].append(("wait", sem, val))
        clk[sem] = val
        snap = self.snaps.get((sem, val))
        if snap:
            for s2, v2 in snap.items():
                if v2 > clk.get(s2, 0):
                    clk[s2] = v2

    def emit(self, q, fns, reads=(), writes=(), dma=False):
        if callable(fns):
            fns = [fns]
        need = []
        for k in reads:
            ev = self.writes.get(k)
            if ev:
                need.append(ev)
        for k in writes:
            ev = self.writes.get(k)
            if ev:
                need.append(ev)
            for q2, ev2 in self.readers.get(k, {}).items():
                if q2 != q:
                    need.append(ev2)
                else:
                    need.append(ev2)
        if dma:
            assert q in self.dma_sems, f"dma emit on non-dma queue {q}"
            i = self.dma_rr[q]
            self.dma_rr[q] = (i + 1) % len(self.dma_sems[q])
            sem = self.dma_sems[q][i]
            prev = self.dma_last_ev.get(sem)
            if prev is not None:
                need.append(prev)
        else:
            sem = f"c_{q}"
        best = {}
        for s2, v2 in need:
            if v2 > best.get(s2, 0):
                best[s2] = v2
        for s2 in sorted(best):
            if not dma and s2 == sem and q == "te":
                continue
            self._wait(q, (s2, best[s2]))
        inc = 16 if dma else 1
        for i, fn in enumerate(fns):
            this_inc = inc if dma or i == len(fns) - 1 else 0
            self.items[q].append(("call", fn, sem if this_inc else None, this_inc))
            self.sem_total[sem] += this_inc
        ev = (sem, self.sem_total[sem])
        if dma:
            self.dma_last_ev[sem] = ev
        clk = dict(self.clocks[q])
        clk[sem] = ev[1]
        self.snaps[ev] = clk
        self.qlast[q] = ev
        for k in reads:
            self.readers.setdefault(k, {})[q] = ev
        for k in writes:
            self.writes[k] = ev
            self.readers[k] = {q: ev}
        return ev

    def finalize(self, tail_queue="pool"):
        nc = self.nc
        for s in self.sem_order:
            self.sems[s] = self.st.enter_context(nc.semaphore(s))
        for q in QUEUES:
            for it in self.items[q]:
                if it[0] == "wait":
                    assert it[2] <= self.sem_total[it[1]], (q, it)
        block = self.st.enter_context(nc.Block())
        for q in QUEUES:
            items = list(self.items[q])
            if q == tail_queue:
                items += [("wait", s, self.sem_total[s]) for s in self.sem_order
                          if self.sem_total[s] > 0 and s != f"c_{q}"]
            sems = self.sems

            def mk(eng_items):
                def body(eng):
                    for it in eng_items:
                        if it[0] == "wait":
                            eng.wait_ge(sems[it[1]], it[2])
                        else:
                            _, fn, sem, inc = it
                            inst = fn(eng)
                            if inc:
                                inst.then_inc(sems[sem], inc)

                return body

            getattr(block, ENG_ATTR[q])(mk(items))

    def close(self):
        self.st.close()




def ceil_div(a, b):
    return (a + b - 1) // b


class Emitter:

    def __init__(self, P: Prog):
        self.P = P
        self.nc = P.nc
        self.uid = 0

    def name(self, base):
        self.uid += 1
        return f"{base}_{self.uid}"

    def load_full(self, dram_ap, shape, dtype=F32, key=None, q="sp"):
        t = self.P.sb(self.name("ld"), shape, dtype)
        key = key or t.name
        self.P.emit(q, lambda e: e.dma_start(out=t[:, :], in_=dram_ap), writes=[key], dma=True)
        return t, key

    def load_pcol(self, dram_vec_ap, n, key=None):
        cols = ceil_div(n, 128)
        t = self.P.sb(self.name("pcol"), [128, cols], F32)
        key = key or t.name
        fns = []
        for j in range(cols):
            lo = j * 128
            hi = min(n, lo + 128)
            src = dram_vec_ap[lo:hi]
            src = AP(src.tensor, src.offset, [[1, hi - lo], [1, 1]])
            fns.append(lambda e, s=src, d=t[0 : hi - lo, j : j + 1]: e.dma_start(
                out=d, in_=s, allow_slow_non_contiguous=True))
        self.P.emit("pool", fns, writes=[key], dma=True)
        return t, key


def bf(x):
    return np.ascontiguousarray(x).astype(BF)


class Em(Emitter):

    def setup_consts(self, ident_dram, gb01_dram):
        P = self.P
        self.gb01_dram = gb01_dram
        self.ident = P.sb("c_ident", [128, 128], BF16)
        P.emit("pool", lambda e: e.dma_start(out=self.ident[:, :], in_=ident_dram), writes=["ident"], dma=True)
        self.ones_col = P.sb("c_onescol", [128, 1], F32)
        P.emit("dve", lambda e: e.memset(self.ones_col[:, :], 1.0), writes=["ones_col"])
        self.ones_col_b = P.sb("c_onescolb", [128, 1], BF16)
        P.emit("dve", lambda e: e.memset(self.ones_col_b[:, :], 1.0), writes=["ones_col"])
        self.ones_row = P.sb("c_onesrow", [1, 128], BF16)
        P.emit("dve", lambda e: e.memset(self.ones_row[:, :], 1.0), writes=["ones_row"])
        self.ones_row_f = P.sb("c_onesrowf", [1, 64], F32)
        P.emit("dve", lambda e: e.memset(self.ones_row_f[:, :], 1.0), writes=["ones_row"])
        self.eps_col = P.sb("c_epscol", [1, 1], F32)
        P.emit("dve", lambda e: e.memset(self.eps_col[:, :], 1e-5), writes=["eps_col"])
        if not getattr(self, "no_gb01", False):
            self.gb01 = P.sb("c_gb01", [2, 1024], F32)
            P.emit("sp", lambda e: e.dma_start(out=self.gb01[:, :], in_=self.gb01_dram), writes=["gb01"], dma=True)
        self.rows_mr = P.sb("c_rowsmr", [2, 512], F32)
        P.emit("dve", lambda e: e.memset(self.rows_mr[0:2, :], 1.0), writes=["rows1", "rows0"])
        self.rstd_row = P.sb("c_rstdrow", [1, 512], F32)
        self.mean_row = P.sb("c_meanrow", [1, 512], F32)
        self.scr = [P.sb(f"c_scr{i}", [128, 512], F32) for i in range(4)]
        self.pb = [P.ps(f"pb{i}", [128, 512], F32) for i in range(4)]
        self.ptr = [P.ps("ptr_bfA", [128, 1024], BF16), P.ps("ptr_bfB", [128, 1024], BF16)]
        ww = getattr(self, "wstg_width", 4096)
        self.wstg = [P.sb(f"c_wstg{i}", [128, ww], BF16) for i in range(2)]
        self._wstg_i = 0

    def wT(self, dram_ap, N, K, dst_tiles=None, dst_key=None, col_off=0):
        P = self.P
        kt = K // 128
        if dst_tiles is None:
            dst_tiles = [P.sb(self.name("wT"), [128, N], BF16) for _ in range(kt)]
            dst_key = dst_key or self.name("wTk")
        assert N <= 4096
        raw = getattr(dram_ap, "ndim", 2) > 2
        if raw:
            assert N == 128
        for n0 in range(0, N, 128):
            stg = self.wstg[self._wstg_i]
            skey = f"wstg{self._wstg_i}"
            self._wstg_i ^= 1
            a_src = dram_ap if raw else dram_ap[n0 : n0 + 128, 0:K]
            P.emit("pool", lambda e, s=stg, a=a_src: e.dma_start(
                out=s[:, 0:K], in_=a), reads=[], writes=[skey], dma=True)
            for ki in range(kt):
                slot = (n0 // 128 + ki) % 2
                pkey = f"ptr{slot}"
                P.emit("te", lambda e, s=stg, ki=ki, slot=slot: e.transpose(
                    self.ptr[slot][:, 0:128], s[:, ki * 128 : ki * 128 + 128], self.ident[:, :]),
                    reads=[skey, "ident"], writes=[pkey])
                P.emit("dve", lambda e, ki=ki, n0=n0, slot=slot, d=dst_tiles[ki]: e.tensor_copy(
                    d[:, col_off + n0 : col_off + n0 + 128], self.ptr[slot][:, 0:128]),
                    reads=[pkey], writes=[dst_key])
        return dst_tiles, dst_key

    def ln(self, xt, xkeys, T, out, okeys, gb=None, gbkey=None, nfeat=1024, Toff=0,
           loader=None, stats_rows=None, stats_key=None, apply=True,
           out_off=0, lo_out=None, lo_keys=None):
        P = self.P
        nct = nfeat // 128
        if gb is None:
            gb, gbkey = self.gb01, "gb01"
        if loader is None:
            def loader(ct, c0, cw):
                sl = slice(Toff + c0, Toff + c0 + cw)
                return xt[ct][:, sl], xkeys[ct]
        sq = [self.scr[0], self.scr[1]]
        tstage = self.scr[2]
        rinv = 1.0 / nfeat
        for c0 in range(0, T, 512):
            cw = min(512, T - c0)
            psA, psB, psC, psD = self.pb[0], self.pb[1], self.pb[2], self.pb[3]
            for ct in range(nct):
                xap, xkey = loader(ct, c0, cw)
                sqt = sq[ct % 2]
                sqk = f"{sqt.name}"
                P.emit("act", lambda e, s=sqt, xap=xap, cw=cw: e.activation(
                    s[:, 0:cw], xap, AF.Square), reads=[xkey], writes=[sqk])
                P.emit("te", lambda e, ct=ct, xap=xap, cw=cw: e.matmul(
                    psA[0:1, 0:cw], self.ones_col[:, :], xap,
                    start=(ct == 0), stop=(ct == nct - 1)),
                    reads=[xkey, "ones_col"], writes=[psA.name])
                P.emit("te", lambda e, ct=ct, s=sqt, cw=cw: e.matmul(
                    psB[0:1, 0:cw], self.ones_col[:, :], s[:, 0:cw],
                    start=(ct == 0), stop=(ct == nct - 1)),
                    reads=[sqk, "ones_col"], writes=[psB.name])
            mr, rs, mn = self.rows_mr, self.rstd_row, self.mean_row
            P.emit("dve", lambda e, cw=cw: e.tensor_scalar(mn[0:1, 0:cw], psA[0:1, 0:cw], rinv, None, ALU.mult),
                   reads=[psA.name], writes=["mean_row"])
            P.emit("dve", lambda e, cw=cw: e.tensor_scalar(rs[0:1, 0:cw], psB[0:1, 0:cw], rinv, None, ALU.mult),
                   reads=[psB.name], writes=["rstd_row"])
            P.emit("dve", lambda e, cw=cw: e.tensor_tensor(sq[0][0:1, 0:cw], mn[0:1, 0:cw], mn[0:1, 0:cw], ALU.mult),
                   reads=["mean_row"], writes=[sq[0].name])
            P.emit("dve", lambda e, cw=cw: e.tensor_tensor(rs[0:1, 0:cw], rs[0:1, 0:cw], sq[0][0:1, 0:cw], ALU.subtract),
                   reads=["rstd_row", sq[0].name], writes=["rstd_row"])
            P.emit("act", lambda e, cw=cw: e.activation(rs[0:1, 0:cw], rs[0:1, 0:cw], AF.Ln, bias=self.eps_col[:, :]),
                   reads=["rstd_row", "eps_col"], writes=["rstd_row"])
            P.emit("act", lambda e, cw=cw: e.activation(rs[0:1, 0:cw], rs[0:1, 0:cw], AF.Exp, scale=-0.5),
                   reads=["rstd_row"], writes=["rstd_row"])
            P.emit("dve", lambda e, cw=cw: e.tensor_tensor(mr[0:1, 0:cw], mn[0:1, 0:cw], rs[0:1, 0:cw], ALU.mult),
                   reads=["mean_row", "rstd_row"], writes=["rows0"])
            P.emit("dve", lambda e, cw=cw: e.tensor_scalar(mr[0:1, 0:cw], mr[0:1, 0:cw], -1.0, None, ALU.mult),
                   reads=["rows0"], writes=["rows0"])
            if stats_rows is not None:
                P.emit("dve", lambda e, c0=c0, cw=cw: e.tensor_copy(
                    stats_rows[0][0:1, c0 : c0 + cw], mr[0:1, 0:cw]), reads=["rows0"], writes=[stats_key])
                P.emit("dve", lambda e, c0=c0, cw=cw: e.tensor_copy(
                    stats_rows[1][0:1, c0 : c0 + cw], rs[0:1, 0:cw]), reads=["rstd_row"], writes=[stats_key])
            if not apply:
                continue
            for ct in range(nct):
                xap, xkey = loader(ct, c0, cw)
                P.emit("te", lambda e, ct=ct, cw=cw: e.matmul(
                    psC[:, 0:cw], gb[0:1, ct * 128 : ct * 128 + 128], self.rstd_row[0:1, 0:cw], start=True, stop=True),
                    reads=[gbkey, "rstd_row"], writes=[psC.name])
                P.emit("te", lambda e, ct=ct, cw=cw: e.matmul(
                    psD[:, 0:cw], gb[0:2, ct * 128 : ct * 128 + 128], self.rows_mr[0:2, 0:cw], start=True, stop=True),
                    reads=[gbkey, "rows0", "rows1"], writes=[psD.name])
                P.emit("dve", lambda e, xap=xap, cw=cw: e.tensor_tensor(
                    tstage[:, 0:cw], xap, psC[:, 0:cw], ALU.mult),
                    reads=[xkey, psC.name], writes=[tstage.name])
                if lo_out is None:
                    P.emit("dve", lambda e, ct=ct, c0=c0, cw=cw: e.tensor_tensor(
                        out[ct][:, out_off + c0 : out_off + c0 + cw], tstage[:, 0:cw],
                        psD[:, 0:cw], ALU.add), reads=[tstage.name, psD.name], writes=[okeys[ct]])
                else:
                    t2 = self.scr[3]
                    P.emit("dve", lambda e, cw=cw: e.tensor_tensor(
                        t2[:, 0:cw], tstage[:, 0:cw], psD[:, 0:cw], ALU.add),
                        reads=[tstage.name, psD.name], writes=[t2.name])
                    P.emit("dve", lambda e, ct=ct, c0=c0, cw=cw: e.tensor_copy(
                        out[ct][:, out_off + c0 : out_off + c0 + cw], t2[:, 0:cw]),
                        reads=[t2.name], writes=[okeys[ct]])
                    P.emit("dve", lambda e, ct=ct, c0=c0, cw=cw: e.tensor_tensor(
                        lo_out[ct][:, c0 : c0 + cw], t2[:, 0:cw],
                        out[ct][:, out_off + c0 : out_off + c0 + cw], ALU.subtract),
                        reads=[t2.name, okeys[ct]], writes=[lo_keys[ct]])



TH = 2304
TC = 2048
PEN = -60000.0




TH = 2304
TC = 2048
PEN = -60000.0


def chunks(total, step=512):
    return [(c0, min(step, total - c0)) for c0 in range(0, total, step)]


def build_L0(stage='full'):
    nc = bass.Bass("TRN2", target_bir_lowering=False)
    d_x = nc.dram_tensor("x", [1024, TH], F32, kind="ExternalInput").ap()
    d_pen = nc.dram_tensor("pen", [1, TH], F32, kind="ExternalInput").ap()
    d_ident = nc.dram_tensor("ident", [128, 128], F32, kind="ExternalInput").ap()
    d_gb01 = nc.dram_tensor("gb01", [2, 1024], F32, kind="ExternalInput").ap()
    d_ln1 = nc.dram_tensor("ln1", [2, 1024], F32, kind="ExternalInput").ap()
    d_ln2 = nc.dram_tensor("ln2", [2, 1024], F32, kind="ExternalInput").ap()
    d_qkvw = nc.dram_tensor("qkv_w", [3072, 1024], F32, kind="ExternalInput").ap()
    d_qkvb = nc.dram_tensor("qkv_b", [1, 3072], F32, kind="ExternalInput").ap()
    d_projw = nc.dram_tensor("proj_w", [1024, 1024], F32, kind="ExternalInput").ap()
    d_projb = nc.dram_tensor("proj_b", [1, 1024], F32, kind="ExternalInput").ap()
    d_fc1w = nc.dram_tensor("fc1_w", [4096, 1024], F32, kind="ExternalInput").ap()
    d_fc1b = nc.dram_tensor("fc1_b", [1, 4096], F32, kind="ExternalInput").ap()
    d_fc2w = nc.dram_tensor("fc2_w", [1024, 4096], F32, kind="ExternalInput").ap()
    d_fc2b = nc.dram_tensor("fc2_b", [1, 1024], F32, kind="ExternalInput").ap()
    d_ffnw = nc.dram_tensor("ffn_w", [1024, 1024], F32, kind="ExternalInput").ap()
    d_ffnb = nc.dram_tensor("ffn_b", [1, 1024], F32, kind="ExternalInput").ap()
    d_xo = nc.dram_tensor("xo", [1024, TC], F32, kind="ExternalOutput").ap()

    P = Prog(nc)
    em = Em(P)
    em.wstg_width = 1024
    em.no_gb01 = True
    em.setup_consts(d_ident, d_gb01)
    pb = em.pb

    xstg = [P.sb(f"xstg{i}", [128, 512], F32) for i in range(2)]
    xstg_i = [0]

    def xload(ct, c0, cw, base=0):
        i = xstg_i[0]
        xstg_i[0] = (i + 1) % 2
        t = xstg[i]
        P.emit("sp", lambda e, t=t, ct=ct, c0=c0, cw=cw: e.dma_start(
            out=t[:, 0:cw], in_=d_x[ct * 128 : (ct + 1) * 128, base + c0 : base + c0 + cw]),
            writes=[t.name], dma=True)
        return t[:, 0:cw], t.name

    xb = [P.sb(f"xb{i}", [128, TH], F32) for i in range(8)]
    xbk = [f"xb{i}" for i in range(8)]
    ar1 = [P.sb(f"ar1_{i}", [128, TH], BF16) for i in range(8)]
    ar1k = [f"ar1_{i}" for i in range(8)]
    ar2 = [P.sb(f"ar2_{i}", [128, TC], BF16) for i in range(8)]
    ar2k = [f"ar2_{i}" for i in range(8)]
    ar3 = [P.sb(f"ar3_{i}", [128, 1024], BF16) for i in range(8)]
    ar3k = [f"ar3_{i}" for i in range(8)]

    gbw = P.sb("gbw", [2, 1024], F32)
    P.emit("sp", lambda e: e.dma_start(out=gbw[:, :], in_=d_gb01), writes=["gbw"], dma=True)
    em.ln(None, None, TH, xb, xbk, loader=xload, gb=gbw, gbkey="gbw")
    P.emit("sp", lambda e: e.dma_start(out=gbw[:, :], in_=d_ln1), writes=["gbw"], dma=True)
    xl, xlk = ar1, ar1k
    em.ln(xb, xbk, TH, xl, xlk, gb=gbw, gbkey="gbw")

    if stage == 'xl':
        for ct in range(8):
            P.emit("dve", lambda e, ct=ct: e.tensor_copy(xb[ct][:, :], xl[ct][:, :]),
                   reads=[ar1k[ct]], writes=[xbk[ct]])
            P.emit("sp", lambda e, ct=ct: e.dma_start(
                out=d_xo[ct * 128 : (ct + 1) * 128, :], in_=xb[ct][:, 128 : 128 + TC]),
                reads=[xbk[ct]], writes=[], dma=True)
        P.finalize()
        return nc, P

    penrows = P.sb("penrows", [2, TH], BF16)
    P.emit("dve", lambda e: e.memset(penrows[0:2, :], 1.0), writes=["penrows"])
    P.emit("pool", lambda e: e.dma_start(out=penrows[1:2, :], in_=d_pen), writes=["penrows"], dma=True)
    boundrhs = P.sb("boundrhs", [2, TC], BF16)
    P.emit("dve", lambda e: e.memset(boundrhs[0:2, :], PEN), writes=["boundrhs_c"])
    brow = P.sb("brow", [1, 768], BF16)
    qkvb_pc, _ = em.load_pcol(d_qkvb.rearrange("a b -> (a b)"), 3072, key="qkvb_pc")

    kt_h = P.sb("kt_h", [64, TH], BF16)
    qt_h = P.sb("qt_h", [64, TC], BF16)
    vh = P.sb("vh", [128, 18 * 64], BF16)
    sqstg = [em.scr[0], em.scr[1]]
    kmax2 = P.sb("kmax2", [1, 2], F32)
    probs = P.sb("probs", [128, 384], BF16)
    rrow = P.sb("rrow", [1, 128], F32)
    rbc = P.sb("rbc", [64, 128], F32)
    aoT, aok = ar2, ar2k

    wq = [ar3[i].ap()[:, 0:256] for i in range(8)]
    wk_ = [ar3[i].ap()[:, 256:512] for i in range(8)]
    wv = [ar3[i].ap()[:, 512:768] for i in range(8)]

    for hp in range(4):
        P.emit("pool", lambda e, hp=hp: e.dma_start(
            out=brow[0:1, 0:256], in_=d_qkvb[0:1, 2048 + hp * 256 : 2304 + hp * 256]),
            writes=["qb_bf"], dma=True)
        em.wT(d_qkvw[hp * 256 : (hp + 1) * 256, :], 256, 1024, dst_tiles=wq, dst_key="wqk")
        em.wT(d_qkvw[1024 + hp * 256 : 1280 + hp * 256, :], 256, 1024, dst_tiles=wk_, dst_key="wqk")
        em.wT(d_qkvw[2048 + hp * 256 : 2304 + hp * 256, :], 256, 1024, dst_tiles=wv, dst_key="wqk")
        for h4 in range(4):
            h = hp * 4 + h4
            co = h4 * 64
            for c0, cw in chunks(TH):
                fns = [
                    (lambda e, k=k, co=co, c0=c0, cw=cw: e.matmul(
                        pb[0][0:64, 0:cw], wk_[k][:, co : co + 64], xl[k][:, c0 : c0 + cw],
                        start=(k == 0), stop=False)) for k in range(8)]
                fns[-1] = (lambda e, co=co, c0=c0, cw=cw: e.matmul(
                    pb[0][0:64, 0:cw], wk_[7][:, co : co + 64], xl[7][:, c0 : c0 + cw],
                    start=False, stop=True))
                P.emit("te", fns, reads=["wqk"] + xlk, writes=[pb[0].name])
                kb_col = qkvb_pc[(h % 2) * 64 : (h % 2) * 64 + 64, 8 + h // 2 : 9 + h // 2]
                P.emit("dve", lambda e, c0=c0, cw=cw, kb_col=kb_col: e.tensor_scalar(
                    kt_h[:, c0 : c0 + cw], pb[0][0:64, 0:cw], kb_col, None, ALU.add),
                    reads=[pb[0].name, "qkvb_pc"], writes=["kt_h"])
            for c0, cw in chunks(TC):
                fns = [
                    (lambda e, k=k, co=co, c0=c0, cw=cw: e.matmul(
                        pb[1][0:64, 0:cw], wq[k][:, co : co + 64], xl[k][:, 128 + c0 : 128 + c0 + cw],
                        start=(k == 0), stop=False)) for k in range(8)]
                fns[-1] = (lambda e, co=co, c0=c0, cw=cw: e.matmul(
                    pb[1][0:64, 0:cw], wq[7][:, co : co + 64], xl[7][:, 128 + c0 : 128 + c0 + cw],
                    start=False, stop=True))
                P.emit("te", fns, reads=["wqk"] + xlk, writes=[pb[1].name])
                qb_col = qkvb_pc[(h % 2) * 64 : (h % 2) * 64 + 64, h // 2 : h // 2 + 1]
                P.emit("dve", lambda e, c0=c0, cw=cw, qb_col=qb_col: e.tensor_scalar(
                    qt_h[:, c0 : c0 + cw], pb[1][0:64, 0:cw], qb_col, 0.125, ALU.add, ALU.mult),
                    reads=[pb[1].name, "qkvb_pc"], writes=["qt_h"])
            for ti in range(18):
                fns = [
                    (lambda e, k=k, ti=ti, co=co: e.matmul(
                        pb[2][:, 0:64], xl[k][:, ti * 128 : (ti + 1) * 128], wv[k][:, co : co + 64],
                        start=(k == 0), stop=False)) for k in range(8)]
                fns.append(lambda e, co=co: e.matmul(
                    pb[2][:, 0:64], em.ones_row[0:1, 0:128],
                    brow[0:1, co : co + 64], start=False, stop=True))
                P.emit("te", fns, reads=["wqk", "qb_bf", "ones_row"] + xlk, writes=[pb[2].name])
                P.emit("dve", lambda e, ti=ti: e.tensor_copy(
                    vh[:, ti * 64 : (ti + 1) * 64], pb[2][:, 0:64]), reads=[pb[2].name], writes=["vh"])
            P.emit("dve", lambda e: e.memset(kmax2[0:1, 0:2], 0.0), writes=["kmax2", "kmax2b"])
            for c0, cw in chunks(TH):
                sg = sqstg[(c0 // 512) % 2]
                P.emit("act", lambda e, sg=sg, c0=c0, cw=cw: e.activation(
                    sg[0:64, 0:cw], kt_h[:, c0 : c0 + cw], AF.Square), reads=["kt_h"], writes=[sg.name])
                P.emit("te", lambda e, sg=sg, cw=cw: e.matmul(
                    pb[3][0:1, 0:cw], em.ones_col[0:64, :], sg[0:64, 0:cw], start=True, stop=True),
                    reads=[sg.name, "ones_col"], writes=[pb[3].name])
                P.emit("dve", lambda e, cw=cw: e.tensor_reduce(
                    kmax2[0:1, 1:2], pb[3][0:1, 0:cw], AX.X, ALU.max), reads=[pb[3].name], writes=["kmax2b"])
                P.emit("dve", lambda e: e.tensor_tensor(
                    kmax2[0:1, 0:1], kmax2[0:1, 0:1], kmax2[0:1, 1:2], ALU.max),
                    reads=["kmax2", "kmax2b"], writes=["kmax2"])
            for c0, cw in chunks(TC):
                sg = sqstg[(c0 // 512) % 2]
                brw = em.scr[3]
                P.emit("act", lambda e, sg=sg, c0=c0, cw=cw: e.activation(
                    sg[0:64, 0:cw], qt_h[:, c0 : c0 + cw], AF.Square), reads=["qt_h"], writes=[sg.name])
                P.emit("te", lambda e, sg=sg, cw=cw: e.matmul(
                    pb[3][0:1, 0:cw], em.ones_col[0:64, :], sg[0:64, 0:cw], start=True, stop=True),
                    reads=[sg.name, "ones_col"], writes=[pb[3].name])
                P.emit("dve", lambda e, brw=brw, cw=cw: e.tensor_scalar(
                    brw[0:1, 0:cw], pb[3][0:1, 0:cw], kmax2[0:1, 0:1], 1e-30, ALU.mult, ALU.max),
                    reads=[pb[3].name, "kmax2"], writes=[brw.name])
                P.emit("act", lambda e, brw=brw, cw=cw: e.activation(
                    brw[0:1, 0:cw], brw[0:1, 0:cw], AF.Ln), reads=[brw.name], writes=[brw.name])
                P.emit("act", lambda e, brw=brw, cw=cw: e.activation(
                    brw[0:1, 0:cw], brw[0:1, 0:cw], AF.Exp, scale=0.5), reads=[brw.name], writes=[brw.name])
                P.emit("dve", lambda e, brw=brw, c0=c0, cw=cw: e.tensor_scalar(
                    boundrhs[0:1, c0 : c0 + cw], brw[0:1, 0:cw], -1.02, None, ALU.mult),
                    reads=[brw.name], writes=["boundrhs"])
            for w in range(16):
                fns = []
                for j in range(3):
                    kb = w + j
                    fns.append(lambda e, j=j, kb=kb, w=w: e.matmul(
                        pb[3][:, j * 128 : j * 128 + 128], kt_h[:, kb * 128 : (kb + 1) * 128],
                        qt_h[:, w * 128 : (w + 1) * 128], start=True, stop=False))
                    fns.append(lambda e, j=j, kb=kb, w=w: e.matmul(
                        pb[3][:, j * 128 : j * 128 + 128], penrows[0:2, kb * 128 : (kb + 1) * 128],
                        boundrhs[0:2, w * 128 : (w + 1) * 128], start=False, stop=True))
                P.emit("te", fns, reads=["kt_h", "qt_h", "penrows", "boundrhs", "boundrhs_c"],
                       writes=[pb[3].name])
                P.emit("act", lambda e: e.activation(probs[:, 0:384], pb[3][:, 0:384], AF.Exp),
                       reads=[pb[3].name], writes=["probs"])
                fns = [(lambda e, j=j: e.matmul(
                    pb[2][0:1, 0:128], em.ones_col_b[:, :], probs[:, j * 128 : (j + 1) * 128],
                    start=(j == 0), stop=(j == 2))) for j in range(3)]
                P.emit("te", fns, reads=["probs", "ones_col"], writes=[pb[2].name])
                fns = [(lambda e, j=j, w=w: e.matmul(
                    pb[1][0:64, 0:128], vh[:, (w + j) * 64 : (w + j + 1) * 64],
                    probs[:, j * 128 : (j + 1) * 128], start=(j == 0), stop=(j == 2))) for j in range(3)]
                P.emit("te", fns, reads=["probs", "vh"], writes=[pb[1].name])
                P.emit("dve", lambda e: e.reciprocal(rrow[0:1, :], pb[2][0:1, 0:128]),
                       reads=[pb[2].name], writes=["rrow"])
                P.emit("te", lambda e: e.matmul(
                    pb[0][0:64, 0:128], em.ones_row_f[0:1, 0:64], rrow[0:1, :], start=True, stop=True),
                    reads=["rrow", "ones_row"], writes=[pb[0].name])
                P.emit("dve", lambda e: e.tensor_copy(rbc[:, :], pb[0][0:64, 0:128]),
                       reads=[pb[0].name], writes=["rbc"])
                tile = aoT[h // 2]
                ro = (h % 2) * 64
                P.emit("dve", lambda e, w=w, ro=ro, tile=tile: e.tensor_tensor(
                    tile[ro : ro + 64, w * 128 : (w + 1) * 128], pb[1][0:64, 0:128], rbc[:, :], ALU.mult),
                    reads=[pb[1].name, "rbc"], writes=[aok[h // 2]])

    if stage == 'ao':
        for ct in range(8):
            P.emit("dve", lambda e, ct=ct: e.tensor_copy(xb[ct][:, 128 : 128 + TC], aoT[ct][:, :]),
                   reads=[aok[ct]], writes=[xbk[ct]])
            P.emit("sp", lambda e, ct=ct: e.dma_start(
                out=d_xo[ct * 128 : (ct + 1) * 128, :], in_=xb[ct][:, 128 : 128 + TC]),
                reads=[xbk[ct]], writes=[], dma=True)
        P.finalize()
        return nc, P

    projT = [ar3[i].ap()[:, 0:1024] for i in range(8)]
    em.wT(d_projw, 1024, 1024, dst_tiles=projT, dst_key="wqk")
    projb_pc, _ = em.load_pcol(d_projb.rearrange("a b -> (a b)"), 1024, key="projb_pc")
    for co in range(8):
        for c0, cw in chunks(TC):
            pbank = pb[co % 2]
            fns = [(lambda e, k=k, co=co, c0=c0, cw=cw, pbank=pbank: e.matmul(
                pbank[:, 0:cw], projT[k][:, co * 128 : (co + 1) * 128], aoT[k][:, c0 : c0 + cw],
                start=(k == 0), stop=False)) for k in range(8)]
            fns[-1] = (lambda e, k=7, co=co, c0=c0, cw=cw, pbank=pbank: e.matmul(
                pbank[:, 0:cw], projT[7][:, co * 128 : (co + 1) * 128], aoT[7][:, c0 : c0 + cw],
                start=False, stop=True))
            P.emit("te", fns, reads=["wqk"] + aok, writes=[pbank.name])
            P.emit("dve", lambda e, co=co, c0=c0, cw=cw, pbank=pbank: e.scalar_tensor_tensor(
                xb[co][:, 128 + c0 : 128 + c0 + cw], pbank[:, 0:cw], projb_pc[:, co : co + 1],
                xb[co][:, 128 + c0 : 128 + c0 + cw], ALU.add, ALU.add),
                reads=[xbk[co], pbank.name, "projb_pc"], writes=[xbk[co]])

    if stage == 'proj':
        for ct in range(8):
            P.emit("sp", lambda e, ct=ct: e.dma_start(
                out=d_xo[ct * 128 : (ct + 1) * 128, :], in_=xb[ct][:, 128 : 128 + TC]),
                reads=[xbk[ct]], writes=[], dma=True)
        P.finalize()
        return nc, P

    P.emit("sp", lambda e: e.dma_start(out=gbw[:, :], in_=d_ln2), writes=["gbw"], dma=True)
    em.ln(xb, xbk, TC, xl, xlk, gb=gbw, gbkey="gbw", Toff=128)


    fc2b_pc, _ = em.load_pcol(d_fc2b.rearrange("a b -> (a b)"), 1024, key="fc2b_pc")
    fc1c = [ar3[i].ap()[:, 0:256] for i in range(8)]
    h1, h1k = ar2, ar2k
    fc1b_pc, _ = em.load_pcol(d_fc1b.rearrange("a b -> (a b)"), 4096, key="fc1b_pc")
    for fchunk in range(16):
        em.wT(d_fc1w[fchunk * 256 : (fchunk + 1) * 256, :], 256, 1024, dst_tiles=fc1c, dst_key="wqk")
        fc2ct = [ar2[2 + j].ap()[:, 0:1024] for j in range(2)]
        em.wT(d_fc2w[:, fchunk * 256 : (fchunk + 1) * 256], 1024, 256, dst_tiles=fc2ct, dst_key="fc2ck")
        for dt2 in range(2):
            for c0, cw in chunks(TC):
                pbank = pb[dt2 % 2]
                fns = [(lambda e, k=k, dt2=dt2, c0=c0, cw=cw, pbank=pbank: e.matmul(
                    pbank[:, 0:cw], fc1c[k][:, dt2 * 128 : (dt2 + 1) * 128], xl[k][:, c0 : c0 + cw],
                    start=(k == 0), stop=False)) for k in range(8)]
                fns[-1] = (lambda e, dt2=dt2, c0=c0, cw=cw, pbank=pbank: e.matmul(
                    pbank[:, 0:cw], fc1c[7][:, dt2 * 128 : (dt2 + 1) * 128], xl[7][:, c0 : c0 + cw],
                    start=False, stop=True))
                P.emit("te", fns, reads=["wqk"] + xlk, writes=[pbank.name])
                P.emit("act", lambda e, dt2=dt2, c0=c0, cw=cw, pbank=pbank, fchunk=fchunk: e.activation(
                    h1[dt2][:, c0 : c0 + cw], pbank[:, 0:cw], AF.Gelu,
                    bias=fc1b_pc[:, fchunk * 2 + dt2 : fchunk * 2 + dt2 + 1]),
                    reads=[pbank.name, "fc1b_pc"], writes=[h1k[dt2]])
        for co in range(8):
            for c0, cw in chunks(TC):
                pbank = pb[2 + co % 2]
                fns = [(lambda e, k=k, co=co, c0=c0, cw=cw, pbank=pbank, fc2ct=fc2ct: e.matmul(
                    pbank[:, 0:cw], fc2ct[k][:, co * 128 : (co + 1) * 128], h1[k][:, c0 : c0 + cw],
                    start=(k == 0), stop=(k == 1))) for k in range(2)]
                P.emit("te", fns, reads=["fc2ck"] + h1k[0:2], writes=[pbank.name])
                if fchunk == 0:
                    P.emit("dve", lambda e, co=co, c0=c0, cw=cw, pbank=pbank: e.scalar_tensor_tensor(
                        xb[co][:, 128 + c0 : 128 + c0 + cw], pbank[:, 0:cw], fc2b_pc[:, co : co + 1],
                        xb[co][:, 128 + c0 : 128 + c0 + cw], ALU.add, ALU.add),
                        reads=[xbk[co], pbank.name, "fc2b_pc"], writes=[xbk[co]])
                else:
                    P.emit("dve", lambda e, co=co, c0=c0, cw=cw, pbank=pbank: e.tensor_tensor(
                        xb[co][:, 128 + c0 : 128 + c0 + cw], xb[co][:, 128 + c0 : 128 + c0 + cw],
                        pbank[:, 0:cw], ALU.add), reads=[xbk[co], pbank.name], writes=[xbk[co]])

    for ct in range(8):
        for c0, cw in chunks(TC):
            xap, xkey = xload(ct, c0, cw, base=128)
            P.emit("dve", lambda e, ct=ct, c0=c0, cw=cw, xap=xap: e.tensor_tensor(
                xb[ct][:, 128 + c0 : 128 + c0 + cw], xb[ct][:, 128 + c0 : 128 + c0 + cw],
                xap, ALU.add), reads=[xbk[ct], xkey], writes=[xbk[ct]])

    if stage == 'x1':
        for ct in range(8):
            P.emit("sp", lambda e, ct=ct: e.dma_start(
                out=d_xo[ct * 128 : (ct + 1) * 128, :], in_=xb[ct][:, 128 : 128 + TC]),
                reads=[xbk[ct]], writes=[], dma=True)
        P.finalize()
        return nc, P

    q_sc = P.sb("q_sc", [128, 16], F32)
    q_rs = P.sb("q_rs", [128, 16], F32)
    q_u, q_au, q_a, q_b = em.scr[0], em.scr[1], em.scr[2], em.scr[3]
    q_i = q_au.ap().bitcast(mybir.dt.int32)

    def quant_tile(src_ap, src_key, dst_ap, dst_key):
        s3 = AP(q_sc.ap().tensor, 0, [[16, 128], [1, 16], [0, 32]])
        r3 = AP(q_rs.ap().tensor, 0, [[16, 128], [1, 16], [0, 32]])
        src3 = src_ap.rearrange("p (b i) -> p b i", b=16)
        u3 = q_u.ap().rearrange("p (b i) -> p b i", b=16)
        P.emit("dve", lambda e: e.tensor_reduce(
            q_sc.ap().rearrange("p (b i) -> p b i", b=16, i=1), src3, AX.X, ALU.max,
            apply_absolute_value=True), reads=[src_key], writes=["q_sc"])
        P.emit("dve", lambda e: e.tensor_scalar(q_sc[:, :], q_sc[:, :], 1.0 / 7.5, 1e-30, ALU.mult, ALU.max),
               reads=["q_sc"], writes=["q_sc"])
        P.emit("dve", lambda e: e.reciprocal(q_rs[:, :], q_sc[:, :]), reads=["q_sc"], writes=["q_rs"])
        P.emit("dve", lambda e: e.tensor_tensor(u3, src3, r3, ALU.mult), reads=[src_key, "q_rs"], writes=[q_u.name])
        P.emit("dve", lambda e: e.tensor_tensor(q_au[:, :], q_u[:, :], q_u[:, :], ALU.mult),
               reads=[q_u.name], writes=[q_au.name])
        P.emit("dve", lambda e: e.tensor_single_scalar(q_a[:, :], q_au[:, :], 4.0, ALU.is_gt),
               reads=[q_au.name], writes=[q_a.name])
        P.emit("dve", lambda e: e.tensor_single_scalar(q_b[:, :], q_au[:, :], 16.0, ALU.is_gt),
               reads=[q_au.name], writes=[q_b.name])
        P.emit("dve", lambda e: e.tensor_scalar(q_a[:, :], q_a[:, :], -0.5, 1.0, ALU.mult, ALU.add),
               reads=[q_a.name], writes=[q_a.name])
        P.emit("dve", lambda e: e.tensor_scalar(q_b[:, :], q_b[:, :], -0.5, 1.0, ALU.mult, ALU.add),
               reads=[q_b.name], writes=[q_b.name])
        P.emit("dve", lambda e: e.tensor_tensor(q_a[:, :], q_a[:, :], q_b[:, :], ALU.mult),
               reads=[q_a.name, q_b.name], writes=[q_a.name])
        P.emit("dve", lambda e: e.tensor_scalar(q_a[:, :], q_a[:, :], 8.0, None, ALU.mult),
               reads=[q_a.name], writes=[q_a.name])
        P.emit("dve", lambda e: e.tensor_tensor(q_u[:, :], q_u[:, :], q_a[:, :], ALU.mult),
               reads=[q_u.name, q_a.name], writes=[q_u.name])
        RC = 12582912.0
        P.emit("dve", lambda e: e.tensor_scalar(q_au[:, :], q_u[:, :], RC, None, ALU.add),
               reads=[q_u.name], writes=[q_au.name])
        P.emit("dve", lambda e: e.tensor_scalar(q_u[:, :], q_au[:, :], -RC, None, ALU.add),
               reads=[q_au.name], writes=[q_u.name])
        P.emit("dve", lambda e: e.reciprocal(q_b[:, :], q_a[:, :]), reads=[q_a.name], writes=[q_b.name])
        P.emit("dve", lambda e: e.tensor_tensor(q_u[:, :], q_u[:, :], q_b[:, :], ALU.mult),
               reads=[q_u.name, q_b.name], writes=[q_u.name])
        P.emit("dve", lambda e: e.tensor_tensor(dst_ap.rearrange("p (b i) -> p b i", b=16), u3, s3, ALU.mult),
               reads=[q_u.name, "q_sc"], writes=[dst_key])

    for rt in range(8):
        for half in range(2):
            wr = xstg[half]
            P.emit("sp", lambda e, rt=rt, half=half, wr=wr: e.dma_start(
                out=wr[:, :], in_=d_ffnw[rt * 128 : (rt + 1) * 128, half * 512 : (half + 1) * 512]),
                writes=[wr.name], dma=True)
            quant_tile(wr.ap()[:, 0:512], wr.name,
                       ar2[rt].ap()[:, half * 512 : (half + 1) * 512], ar2k[rt])
    wdT = [ar3[i].ap()[:, 0:1024] for i in range(8)]
    for rt in range(8):
        for ki in range(8):
            slot = (rt * 8 + ki) % 2
            P.emit("te", lambda e, rt=rt, ki=ki, slot=slot: e.transpose(
                em.ptr[slot][:, 0:128], ar2[rt][:, ki * 128 : (ki + 1) * 128], em.ident[:, :]),
                reads=[ar2k[rt], "ident"], writes=[f"ptr{slot}"])
            P.emit("dve", lambda e, rt=rt, ki=ki, slot=slot: e.tensor_copy(
                wdT[ki][:, rt * 128 : (rt + 1) * 128], em.ptr[slot][:, 0:128]),
                reads=[f"ptr{slot}"], writes=["wqk"])

    P.emit("sp", lambda e: e.dma_start(out=gbw[:, :], in_=d_gb01), writes=["gbw"], dma=True)
    em.ln(xb, xbk, TC, xl, xlk, gb=gbw, gbkey="gbw", Toff=128)
    for tt in range(16):
        for ki in range(8):
            slot = (tt * 8 + ki) % 2
            P.emit("te", lambda e, tt=tt, ki=ki, slot=slot: e.transpose(
                em.ptr[slot][:, 0:128], xl[ki][:, tt * 128 : (tt + 1) * 128], em.ident[:, :]),
                reads=[ar1k[ki], "ident"], writes=[f"ptr{slot}"])
            P.emit("dve", lambda e, tt=tt, ki=ki, slot=slot: e.tensor_copy(
                ar2[tt // 2][:, (tt % 2) * 1024 + ki * 128 : (tt % 2) * 1024 + (ki + 1) * 128],
                em.ptr[slot][:, 0:128]), reads=[f"ptr{slot}"], writes=[ar2k[tt // 2]])
    for tt in range(16):
        for half in range(2):
            sl = ar2[tt // 2].ap()[:, (tt % 2) * 1024 + half * 512 : (tt % 2) * 1024 + (half + 1) * 512]
            quant_tile(sl, ar2k[tt // 2], sl, ar2k[tt // 2])
    for tt in range(16):
        for ki in range(8):
            slot = (tt * 8 + ki) % 2
            P.emit("te", lambda e, tt=tt, ki=ki, slot=slot: e.transpose(
                em.ptr[slot][:, 0:128],
                ar2[tt // 2][:, (tt % 2) * 1024 + ki * 128 : (tt % 2) * 1024 + (ki + 1) * 128],
                em.ident[:, :]), reads=[ar2k[tt // 2], "ident"], writes=[f"ptr{slot}"])
            P.emit("dve", lambda e, tt=tt, ki=ki, slot=slot: e.tensor_copy(
                xl[ki][:, tt * 128 : (tt + 1) * 128], em.ptr[slot][:, 0:128]),
                reads=[f"ptr{slot}"], writes=[ar1k[ki]])

    if stage == 'xd':
        for ct in range(8):
            P.emit("dve", lambda e, ct=ct: e.tensor_copy(xb[ct][:, 128 : 128 + TC], xl[ct][:, 0:TC]),
                   reads=[ar1k[ct]], writes=[xbk[ct]])
            P.emit("sp", lambda e, ct=ct: e.dma_start(
                out=d_xo[ct * 128 : (ct + 1) * 128, :], in_=xb[ct][:, 128 : 128 + TC]),
                reads=[xbk[ct]], writes=[], dma=True)
        P.finalize()
        return nc, P
    if stage == 'wd':
        for ct in range(8):
            P.emit("dve", lambda e, ct=ct: e.tensor_copy(xb[ct][:, 128 : 128 + 1024], wdT[ct]),
                   reads=["wqk"], writes=[xbk[ct]])
            P.emit("sp", lambda e, ct=ct: e.dma_start(
                out=d_xo[ct * 128 : (ct + 1) * 128, 0:1024], in_=xb[ct][:, 128 : 128 + 1024]),
                reads=[xbk[ct]], writes=[], dma=True)
        P.finalize()
        return nc, P
    ffnb_pc, _ = em.load_pcol(d_ffnb.rearrange("a b -> (a b)"), 1024, key="ffnb_pc")
    for co in range(8):
        for c0, cw in chunks(TC):
            pbank = pb[co % 2]
            fns = [(lambda e, k=k, co=co, c0=c0, cw=cw, pbank=pbank: e.matmul(
                pbank[:, 0:cw], wdT[k][:, co * 128 : (co + 1) * 128], xl[k][:, c0 : c0 + cw],
                start=(k == 0), stop=False)) for k in range(8)]
            fns[-1] = (lambda e, co=co, c0=c0, cw=cw, pbank=pbank: e.matmul(
                pbank[:, 0:cw], wdT[7][:, co * 128 : (co + 1) * 128], xl[7][:, c0 : c0 + cw],
                start=False, stop=True))
            P.emit("te", fns, reads=["wqk"] + ar1k, writes=[pbank.name])
            P.emit("dve", lambda e, co=co, c0=c0, cw=cw, pbank=pbank: e.scalar_tensor_tensor(
                xb[co][:, 128 + c0 : 128 + c0 + cw], pbank[:, 0:cw], ffnb_pc[:, co : co + 1],
                xb[co][:, 128 + c0 : 128 + c0 + cw], ALU.add, ALU.add),
                reads=[xbk[co], pbank.name, "ffnb_pc"], writes=[xbk[co]])

    for ct in range(8):
        P.emit("sp", lambda e, ct=ct: e.dma_start(
            out=d_xo[ct * 128 : (ct + 1) * 128, :], in_=xb[ct][:, 128 : 128 + TC]),
            reads=[xbk[ct]], writes=[], dma=True)

    P.finalize()
    return nc, P





def build_L1():
    nc = bass.Bass("TRN2", target_bir_lowering=False)
    d_xq = nc.dram_tensor("xq", [1024, 2048], F32, kind="ExternalInput").ap()
    d_xf = nc.dram_tensor("xf", [1024, 4096], F32, kind="ExternalInput").ap()
    d_wq = nc.dram_tensor("wq", [1024, 1024], F32, kind="ExternalInput").ap()
    d_wk = nc.dram_tensor("wk", [1024, 1024], F32, kind="ExternalInput").ap()
    d_wv = nc.dram_tensor("wv", [1024, 1024], F32, kind="ExternalInput").ap()
    d_wo = nc.dram_tensor("wo", [1024, 1024], F32, kind="ExternalInput").ap()
    d_cos = nc.dram_tensor("cosT", [32, 4096], F32, kind="ExternalInput").ap()
    d_sin = nc.dram_tensor("sinT", [32, 4096], F32, kind="ExternalInput").ap()
    d_fr = nc.dram_tensor("Fr", [1024, 1024], F32, kind="ExternalInput").ap()
    d_fi = nc.dram_tensor("Fi", [1024, 1024], F32, kind="ExternalInput").ap()
    d_gr = nc.dram_tensor("Gr", [1024, 1024], F32, kind="ExternalInput").ap()
    d_gi = nc.dram_tensor("Gi", [1024, 1024], F32, kind="ExternalInput").ap()
    d_wfr = nc.dram_tensor("wfr", [1024, 1024], F32, kind="ExternalInput").ap()
    d_wfi = nc.dram_tensor("wfi", [1024, 1024], F32, kind="ExternalInput").ap()
    d_fb = nc.dram_tensor("fb", [1, 1024], F32, kind="ExternalInput").ap()
    d_ident = nc.dram_tensor("ident", [128, 128], F32, kind="ExternalInput").ap()
    d_xo = nc.dram_tensor("xo", [1024, 2048], F32, kind="ExternalOutput").ap()
    s_kt = nc.dram_tensor("s_kt", [16, 64, 4096], BF16, kind="Internal").ap()
    s_v = nc.dram_tensor("s_v", [16, 4096, 65], BF16, kind="Internal").ap()
    s_x1 = nc.dram_tensor("s_x1", [1024, 2048], F32, kind="Internal").ap()

    P = Prog(nc)
    em = Em(P)
    em.wstg_width = 1024
    em.no_gb01 = True
    em.setup_consts(d_ident, None)
    pb = em.pb
    gbw = P.sb("gbw", [2, 1024], F32)
    P.emit("dve", lambda e: e.memset(gbw[0:2, :], 0.0), writes=["gbw"])
    P.emit("dve", lambda e: e.memset(gbw[0:1, :], 1.0), writes=["gbw"])

    A1 = [P.sb(f"l1_A{i}", [128, 1024], BF16) for i in range(8)]
    A2 = [P.sb(f"l1_B{i}", [128, 1024], BF16) for i in range(8)]
    ara = [P.sb(f"l1_ara{i}", [128, 2048], BF16) for i in range(8)]
    C1 = [ara[i].ap()[:, 0:1024] for i in range(8)]
    C2 = [ara[i].ap()[:, 1024:2048] for i in range(8)]
    Mt = [P.sb(f"l1_M{i}", [128, 1024], BF16) for i in range(8)]
    wsl = [P.sb(f"l1_wsl{i}", [128, 128], BF16) for i in range(16)]

    def load8(dst, dram, key):
        for i in range(8):
            P.emit("pool", lambda e, i=i: e.dma_start(
                out=dst[i][:, :], in_=dram[i * 128 : (i + 1) * 128, :]), writes=[key], dma=True)

    load8(A1, d_fr, "l1A")
    load8(A2, d_fi, "l1B")
    for ot in range(8):
        for ft in range(8):
            P.emit("pool", lambda e, ft=ft, ot=ot: e.dma_start(
                out=wsl[ft][:, :], in_=d_wfr[ft * 128 : (ft + 1) * 128, ot * 128 : (ot + 1) * 128]),
                writes=["l1wsl"], dma=True)
            P.emit("pool", lambda e, ft=ft, ot=ot: e.dma_start(
                out=wsl[8 + ft][:, :], in_=d_wfi[ft * 128 : (ft + 1) * 128, ot * 128 : (ot + 1) * 128]),
                writes=["l1wsl"], dma=True)
        for nch in range(2):
            nsl = slice(nch * 512, (nch + 1) * 512)
            pA, pB = pb[0], pb[1]
            P.emit("te", [(lambda e, ft=ft, nsl=nsl, pA=pA: e.matmul(
                pA[:, 0:512], wsl[ft][:, :], A1[ft][:, nsl], start=(ft == 0), stop=(ft == 7)))
                for ft in range(8)], reads=["l1wsl", "l1A"], writes=[pA.name])
            P.emit("te", [(lambda e, ft=ft, nsl=nsl, pB=pB: e.matmul(
                pB[:, 0:512], wsl[8 + ft][:, :], A2[ft][:, nsl], start=(ft == 0), stop=(ft == 7)))
                for ft in range(8)], reads=["l1wsl", "l1B"], writes=[pB.name])
            P.emit("dve", lambda e, ot=ot, nsl=nsl, pA=pA: e.tensor_copy(
                C1[ot][:, nsl], pA[:, 0:512]), reads=[pA.name], writes=["l1ara"])
            P.emit("dve", lambda e, ot=ot, nsl=nsl, pB=pB: e.tensor_tensor(
                C1[ot][:, nsl], C1[ot][:, nsl], pB[:, 0:512], ALU.subtract),
                reads=["l1ara", pB.name], writes=["l1ara"])
            P.emit("te", [(lambda e, ft=ft, nsl=nsl, pA=pA: e.matmul(
                pA[:, 0:512], wsl[8 + ft][:, :], A1[ft][:, nsl], start=(ft == 0), stop=(ft == 7)))
                for ft in range(8)], reads=["l1wsl", "l1A"], writes=[pA.name])
            P.emit("te", [(lambda e, ft=ft, nsl=nsl, pB=pB: e.matmul(
                pB[:, 0:512], wsl[ft][:, :], A2[ft][:, nsl], start=(ft == 0), stop=(ft == 7)))
                for ft in range(8)], reads=["l1wsl", "l1B"], writes=[pB.name])
            P.emit("dve", lambda e, ot=ot, nsl=nsl, pA=pA: e.tensor_copy(
                C2[ot][:, nsl], pA[:, 0:512]), reads=[pA.name], writes=["l1ara"])
            P.emit("dve", lambda e, ot=ot, nsl=nsl, pB=pB: e.tensor_tensor(
                C2[ot][:, nsl], C2[ot][:, nsl], pB[:, 0:512], ALU.add),
                reads=["l1ara", pB.name], writes=["l1ara"])
    for ot in range(8):
        P.emit("dve", lambda e, ot=ot: e.tensor_scalar(
            C2[ot][:, :], C2[ot][:, :], -1.0, None, ALU.mult), reads=["l1ara"], writes=["l1ara"])
    load8(A1, d_gr, "l1A")
    load8(A2, d_gi, "l1B")
    for nt in range(8):
        for mch in range(2):
            msl = slice(mch * 512, (mch + 1) * 512)
            pA = pb[0 if mch == 0 else 1]
            fns = []
            for ot in range(8):
                fns.append(lambda e, ot=ot, nt=nt, msl=msl, pA=pA: e.matmul(
                    pA[:, 0:512], C1[ot][:, nt * 128 : (nt + 1) * 128], A1[ot][:, msl],
                    start=(ot == 0), stop=False))
                fns.append(lambda e, ot=ot, nt=nt, msl=msl, pA=pA: e.matmul(
                    pA[:, 0:512], C2[ot][:, nt * 128 : (nt + 1) * 128], A2[ot][:, msl],
                    start=False, stop=(ot == 7)))
            P.emit("te", fns, reads=["l1ara", "l1ara", "l1A", "l1B"], writes=[pA.name])
            P.emit("dve", lambda e, nt=nt, msl=msl, pA=pA: e.tensor_copy(
                Mt[nt][:, msl], pA[:, 0:512]), reads=[pA.name], writes=["l1M"])

    def wT_perm(dram, dst_tiles, dst_key):
        for t in range(8):
            stg = em.wstg[em._wstg_i]
            skey = f"wstg{em._wstg_i}"
            em._wstg_i ^= 1
            fns = []
            for hh in range(2):
                for eo in range(2):
                    srca = AP(dram.tensor, (t * 128 + hh * 64) * 1024 + eo * 1024,
                              [[2 * 1024, 32], [1, 1024]])
                    dsta = stg[(hh * 2 + eo) * 32 : (hh * 2 + eo + 1) * 32, 0:1024]
                    fns.append(lambda e, srca=srca, dsta=dsta: e.dma_start(out=dsta, in_=srca))
            P.emit("pool", fns, writes=[skey], dma=True)
            for ki in range(8):
                slot = (t * 8 + ki) % 2
                P.emit("te", lambda e, stg=stg, ki=ki, slot=slot: e.transpose(
                    em.ptr[slot][:, 0:128], stg[:, ki * 128 : (ki + 1) * 128], em.ident[:, :]),
                    reads=[skey, "ident"], writes=[f"ptr{slot}"])
                P.emit("dve", lambda e, ki=ki, t=t, slot=slot, d=dst_tiles[ki]: e.tensor_copy(
                    d[:, t * 128 : (t + 1) * 128], em.ptr[slot][:, 0:128]),
                    reads=[f"ptr{slot}"], writes=[dst_key])

    wqT = A1
    wkT = A2
    wvT = C1
    wT_perm(d_wq, wqT, "l1A")
    wT_perm(d_wk, wkT, "l1B")
    em.wT(d_wv, 1024, 1024, dst_tiles=wvT, dst_key="l1ara")

    cosb = P.sb("l1_cos", [32, 4096], BF16)
    sinb = P.sb("l1_sin", [32, 4096], BF16)
    P.emit("pool", lambda e: e.dma_start(out=cosb[:, :], in_=d_cos), writes=["l1cos"], dma=True)
    P.emit("pool", lambda e: e.dma_start(out=sinb[:, :], in_=d_sin), writes=["l1cos"], dma=True)

    lnc = [P.sb(f"l1_lnc{i}", [128, 512], BF16) for i in range(8)]
    lnck = [f"l1_lnc{i}" for i in range(8)]
    xstg = [P.sb(f"l1_xstg{i}", [128, 512], F32) for i in range(2)]
    xstg_i = [0]
    kraw = P.sb("l1_kraw", [64, 512], BF16)
    krp = P.sb("l1_krp", [64, 512], BF16)
    vstg = P.sb("l1_vstg", [128, 65], BF16)
    rpa = P.sb("l1_rpa", [32, 512], BF16)
    rpb = P.sb("l1_rpb", [32, 512], BF16)
    rpc = P.sb("l1_rpc", [32, 512], BF16)
    rpd = P.sb("l1_rpd", [32, 512], BF16)
    P.emit("dve", lambda e: e.memset(vstg[:, :], 1.0), writes=["l1vstg"])
    t_sc = P.sb("l1_tsc", [32, 512], F32)

    def mk_loader(dram, base=0, rkey=None):
        def loader(ct, c0, cw):
            i = xstg_i[0]
            xstg_i[0] = (i + 1) % 2
            t = xstg[i]
            P.emit("sp", lambda e, t=t, ct=ct, c0=c0, cw=cw: e.dma_start(
                out=t[:, 0:cw], in_=dram[ct * 128 : (ct + 1) * 128, base + c0 : base + c0 + cw]),
                reads=[rkey] if rkey else [], writes=[t.name], dma=True)
            return t[:, 0:cw], t.name
        return loader

    def rope(dst, dstkey, src_ap, srckey, ctab, stab, ckey, c0, w, scale=None):
        cs = ctab[:, c0 : c0 + w]
        ss = stab[:, c0 : c0 + w]
        sc = 1.0 if scale is None else scale
        P.emit("dve", lambda e: e.tensor_copy(rpa[:, 0:w], src_ap[0:32, :]),
               reads=[srckey], writes=["l1rpa"])
        P.emit("dve", lambda e: e.tensor_copy(rpb[:, 0:w], src_ap[32:64, :]),
               reads=[srckey], writes=["l1rpb"])
        P.emit("dve", lambda e: e.tensor_tensor(rpc[:, 0:w], rpa[:, 0:w], cs, ALU.mult),
               reads=["l1rpa", ckey], writes=["l1rpc"])
        P.emit("dve", lambda e: e.tensor_tensor(rpd[:, 0:w], rpb[:, 0:w], ss, ALU.mult),
               reads=["l1rpb", ckey], writes=["l1rpd"])
        P.emit("dve", lambda e: e.tensor_tensor(rpc[:, 0:w], rpc[:, 0:w], rpd[:, 0:w], ALU.subtract),
               reads=["l1rpc", "l1rpd"], writes=["l1rpc"])
        P.emit("dve", lambda e, sc=sc: e.tensor_scalar(dst[0:32, :], rpc[:, 0:w], sc, None, ALU.mult),
               reads=["l1rpc"], writes=[dstkey])
        P.emit("dve", lambda e: e.tensor_tensor(rpc[:, 0:w], rpb[:, 0:w], cs, ALU.mult),
               reads=["l1rpb", ckey], writes=["l1rpc"])
        P.emit("dve", lambda e: e.tensor_tensor(rpd[:, 0:w], rpa[:, 0:w], ss, ALU.mult),
               reads=["l1rpa", ckey], writes=["l1rpd"])
        P.emit("dve", lambda e: e.tensor_tensor(rpc[:, 0:w], rpc[:, 0:w], rpd[:, 0:w], ALU.add),
               reads=["l1rpc", "l1rpd"], writes=["l1rpc"])
        P.emit("dve", lambda e, sc=sc: e.tensor_scalar(dst[32:64, :], rpc[:, 0:w], sc, None, ALU.mult),
               reads=["l1rpc"], writes=[dstkey])

    ldf = mk_loader(d_xf)
    for ch in range(8):
        c0 = ch * 512
        em.ln(None, None, 512, lnc, lnck, gb=gbw, gbkey="gbw",
              loader=lambda ct, cc0, cw, c0=c0: ldf(ct, c0 + cc0, cw))
        for h in range(16):
            P.emit("te", [(lambda e, k=k, h=h: e.matmul(
                pb[2][0:64, 0:512], wkT[k][:, h * 64 : (h + 1) * 64], lnc[k][:, :],
                start=(k == 0), stop=(k == 7))) for k in range(8)],
                reads=["l1B"] + lnck, writes=[pb[2].name])
            P.emit("dve", lambda e: e.tensor_copy(kraw[:, :], pb[2][0:64, 0:512]),
                   reads=[pb[2].name], writes=["l1kraw"])
            rope(krp.ap()[:, 0:512], "l1krp", kraw.ap()[:, 0:512], "l1kraw", cosb, sinb, "l1cos", c0, 512)
            P.emit("sp", lambda e, h=h, c0=c0: e.dma_start(
                out=s_kt[h, :, c0 : c0 + 512], in_=krp[:, :]), reads=["l1krp"], writes=["l1skt"], dma=True)
            for tt in range(4):
                P.emit("te", [(lambda e, k=k, h=h, tt=tt: e.matmul(
                    pb[3][:, 0:64], lnc[k][:, tt * 128 : (tt + 1) * 128],
                    wvT[k][:, h * 64 : (h + 1) * 64],
                    start=(k == 0), stop=(k == 7))) for k in range(8)],
                    reads=["l1ara"] + lnck, writes=[pb[3].name])
                P.emit("dve", lambda e: e.tensor_copy(vstg[:, 0:64], pb[3][:, 0:64]),
                       reads=[pb[3].name], writes=["l1vstg"])
                P.emit("sp", lambda e, h=h, c0=c0, tt=tt: e.dma_start(
                    out=s_v[h, c0 + tt * 128 : c0 + (tt + 1) * 128, 0:65], in_=vstg[:, 0:65]),
                    reads=["l1vstg"], writes=["l1sv"], dma=True)

    qta = [P.sb(f"l1_qta{i}", [128, 2048], BF16) for i in range(8)]
    qtak = [f"l1_qta{i}" for i in range(8)]
    ldq = mk_loader(d_xq)
    d_qoff = nc.dram_tensor("coshalf", [32, 2048], F32, kind="ExternalInput").ap()
    d_soff = nc.dram_tensor("sinhalf", [32, 2048], F32, kind="ExternalInput").ap()
    cosq = P.sb("l1_cosq", [32, 2048], BF16)
    sinq = P.sb("l1_sinq", [32, 2048], BF16)
    P.emit("pool", lambda e: e.dma_start(out=cosq[:, :], in_=d_qoff), writes=["l1cosq"], dma=True)
    P.emit("pool", lambda e: e.dma_start(out=sinq[:, :], in_=d_soff), writes=["l1cosq"], dma=True)
    for ch in range(4):
        c0 = ch * 512
        em.ln(None, None, 512, lnc, lnck, gb=gbw, gbkey="gbw",
              loader=lambda ct, cc0, cw, c0=c0: ldq(ct, c0 + cc0, cw))
        for h in range(16):
            P.emit("te", [(lambda e, k=k, h=h: e.matmul(
                pb[2][0:64, 0:512], wqT[k][:, h * 64 : (h + 1) * 64], lnc[k][:, :],
                start=(k == 0), stop=(k == 7))) for k in range(8)],
                reads=["l1A"] + lnck, writes=[pb[2].name])
            P.emit("dve", lambda e: e.tensor_copy(kraw[:, :], pb[2][0:64, 0:512]),
                   reads=[pb[2].name], writes=["l1kraw"])
            dst = qta[h // 2].ap()[(h % 2) * 64 : (h % 2) * 64 + 64, c0 : c0 + 512]
            rope(dst, qtak[h // 2], kraw.ap()[:, 0:512], "l1kraw", cosq, sinq, "l1cosq", c0, 512, scale=0.125)

    woT = A1
    em.wT(d_wo, 1024, 1024, dst_tiles=woT, dst_key="l1A")

    ktb = P.sb("l1_ktb", [64, 4096], BF16)
    vb = [P.sb(f"l1_vb{i}", [128, 65 * 8], BF16) for i in range(4)]
    nbound = P.sb("l1_nbound", [1, 2048], BF16)
    kmax2 = P.sb("l1_kmax2", [1, 2], F32)
    probs = P.sb("l1_probs", [128, 512], BF16)
    qstg = P.sb("l1_qstg", [64, 512], BF16)
    rrow = P.sb("l1_rrow", [1, 512], F32)
    rbc = P.sb("l1_rbc", [64, 512], F32)
    aoT = ara
    aok = ["l1ara"] * 8
    sq64 = em.scr[0]
    brw = em.scr[3]
    pb4 = P.ps("pb4", [64, 512], F32)
    wsb = em.scr[1]

    for h in range(16):
        P.emit("sp", lambda e, h=h: e.dma_start(out=ktb[:, :], in_=s_kt[h, :, :]),
               reads=["l1skt"], writes=["l1ktb"], dma=True)
        for vt in range(4):
            P.emit("sp", lambda e, h=h, vt=vt: e.dma_start(
                out=vb[vt][:, :],
                in_=AP(s_v.tensor, h * 4096 * 65 + vt * 8 * 128 * 65, [[65, 128], [128 * 65, 8], [1, 65]])),
                reads=["l1sv"], writes=["l1vb"], dma=True)
        P.emit("dve", lambda e: e.memset(kmax2[0:1, 0:2], 0.0), writes=["l1kmax", "l1kmaxb"])
        for c0 in range(0, 4096, 512):
            P.emit("act", lambda e, c0=c0: e.activation(sq64[0:64, :], ktb[:, c0 : c0 + 512], AF.Square),
                   reads=["l1ktb"], writes=[sq64.name])
            P.emit("te", lambda e: e.matmul(pb[3][0:1, 0:512], em.ones_col[0:64, :], sq64[0:64, :],
                   start=True, stop=True), reads=[sq64.name, "ones_col"], writes=[pb[3].name])
            P.emit("dve", lambda e: e.tensor_reduce(kmax2[0:1, 1:2], pb[3][0:1, 0:512], AX.X, ALU.max),
                   reads=[pb[3].name], writes=["l1kmaxb"])
            P.emit("dve", lambda e: e.tensor_tensor(kmax2[0:1, 0:1], kmax2[0:1, 0:1],
                   kmax2[0:1, 1:2], ALU.max), reads=["l1kmax", "l1kmaxb"], writes=["l1kmax"])
        for c0 in range(0, 2048, 512):
            P.emit("act", lambda e, h=h, c0=c0: e.activation(
                sq64[0:64, :], qta[h // 2][(h % 2) * 64 : (h % 2) * 64 + 64, c0 : c0 + 512], AF.Square),
                reads=[qtak[h // 2]], writes=[sq64.name])
            P.emit("te", lambda e: e.matmul(pb[3][0:1, 0:512], em.ones_col[0:64, :], sq64[0:64, :],
                   start=True, stop=True), reads=[sq64.name, "ones_col"], writes=[pb[3].name])
            P.emit("dve", lambda e: e.tensor_scalar(brw[0:1, 0:512], pb[3][0:1, 0:512],
                   kmax2[0:1, 0:1], 1e-30, ALU.mult, ALU.max),
                   reads=[pb[3].name, "l1kmax"], writes=[brw.name])
            P.emit("act", lambda e: e.activation(brw[0:1, 0:512], brw[0:1, 0:512], AF.Ln),
                   reads=[brw.name], writes=[brw.name])
            P.emit("act", lambda e: e.activation(brw[0:1, 0:512], brw[0:1, 0:512], AF.Exp, scale=0.5),
                   reads=[brw.name], writes=[brw.name])
            P.emit("dve", lambda e, c0=c0: e.tensor_scalar(nbound[0:1, c0 : c0 + 512],
                   brw[0:1, 0:512], -1.02, None, ALU.mult), reads=[brw.name], writes=["l1nb"])
        for qc in range(4):
            q0 = qc * 512
            pw = pb[0]
            P.emit("dve", lambda e, h=h, q0=q0: e.tensor_copy(
                qstg[:, :], qta[h // 2][(h % 2) * 64 : (h % 2) * 64 + 64, q0 : q0 + 512]),
                reads=[qtak[h // 2]], writes=["l1qstg"])
            for kt_i in range(32):
                ps = pb[1 + kt_i % 2]
                P.emit("te", [
                    (lambda e, kt_i=kt_i, ps=ps: e.matmul(
                        ps[:, 0:512], ktb[:, kt_i * 128 : (kt_i + 1) * 128],
                        qstg[:, :], start=True, stop=False)),
                    (lambda e, q0=q0, ps=ps: e.matmul(
                        ps[:, 0:512], em.ones_row[0:1, 0:128],
                        nbound[0:1, q0 : q0 + 512], start=False, stop=True)),
                ], reads=["l1ktb", "l1qstg", "ones_row", "l1nb"], writes=[ps.name])
                P.emit("act", lambda e, ps=ps: e.activation(probs[:, :], ps[:, 0:512], AF.Exp),
                       reads=[ps.name], writes=["l1probs"])
                P.emit("te", lambda e, kt_i=kt_i, pw=pw: e.matmul(
                    pw[0:65, 0:512], vb[kt_i // 8][:, (kt_i % 8) * 65 : (kt_i % 8 + 1) * 65],
                    probs[:, :], start=(kt_i == 0), stop=(kt_i == 31)),
                    reads=["l1probs", "l1vb"], writes=[pw.name])
            P.emit("dve", lambda e, pw=pw: e.reciprocal(rrow[0:1, :], pw[64:65, 0:512]),
                   reads=[pw.name], writes=["l1rrow"])
            P.emit("te", lambda e: e.matmul(pb[3][0:64, 0:512], em.ones_row_f[0:1, 0:64],
                   rrow[0:1, :], start=True, stop=True), reads=["l1rrow", "ones_row"], writes=[pb[3].name])
            P.emit("dve", lambda e: e.tensor_copy(rbc[:, :], pb[3][0:64, 0:512]),
                   reads=[pb[3].name], writes=["l1rbc"])
            P.emit("dve", lambda e, h=h, q0=q0, pw=pw: e.tensor_tensor(
                aoT[h // 2][(h % 2) * 64 : (h % 2) * 64 + 64, q0 : q0 + 512],
                pw[0:64, 0:512], rbc[:, :], ALU.mult),
                reads=[pw.name, "l1rbc"], writes=[aok[h // 2]])

    x1stg = P.sb("l1_x1stg", [128, 512], F32)
    for co in range(8):
        for c0 in range(0, 2048, 512):
            pbank = pb[co % 2]
            P.emit("te", [(lambda e, k=k, co=co, c0=c0, pbank=pbank: e.matmul(
                pbank[:, 0:512], woT[k][:, co * 128 : (co + 1) * 128], aoT[k][:, c0 : c0 + 512],
                start=(k == 0), stop=(k == 7))) for k in range(8)],
                reads=["l1A", "l1ara"], writes=[pbank.name])
            xap, xkey = ldq(co, c0, 512)
            P.emit("dve", lambda e, pbank=pbank, xap=xap: e.tensor_tensor(
                x1stg[:, :], pbank[:, 0:512], xap, ALU.add),
                reads=[pbank.name, xkey], writes=["l1x1stg"])
            P.emit("sp", lambda e, co=co, c0=c0: e.dma_start(
                out=s_x1[co * 128 : (co + 1) * 128, c0 : c0 + 512], in_=x1stg[:, :]),
                reads=["l1x1stg"], writes=["l1sx1"], dma=True)

    x2ln = qta
    x2lnk = qtak
    ldx1 = mk_loader(s_x1, rkey="l1sx1")
    em.ln(None, None, 2048, x2ln, x2lnk, gb=gbw, gbkey="gbw", loader=ldx1)
    fb_pc, _ = em.load_pcol(d_fb.rearrange("a b -> (a b)"), 1024, key="l1fbpc")
    for mt in range(8):
        for c0 in range(0, 2048, 512):
            pbank = pb[2 + mt % 2]
            P.emit("te", [(lambda e, k=k, mt=mt, c0=c0, pbank=pbank: e.matmul(
                pbank[:, 0:512], Mt[k][:, mt * 128 : (mt + 1) * 128], x2ln[k][:, c0 : c0 + 512],
                start=(k == 0), stop=(k == 7))) for k in range(8)],
                reads=["l1M"] + x2lnk, writes=[pbank.name])
            xap, xkey = ldx1(mt, c0, 512)
            P.emit("dve", lambda e, mt=mt, pbank=pbank, xap=xap: e.scalar_tensor_tensor(
                x1stg[:, :], pbank[:, 0:512], fb_pc[:, mt : mt + 1], xap, ALU.add, ALU.add),
                reads=[pbank.name, xkey, "l1fbpc", "l1sx1"], writes=["l1x1stg"])
            P.emit("sp", lambda e, mt=mt, c0=c0: e.dma_start(
                out=d_xo[mt * 128 : (mt + 1) * 128, c0 : c0 + 512], in_=x1stg[:, :]),
                reads=["l1x1stg"], writes=[], dma=True)

    P.finalize()
    return nc, P



def dft_consts():
    n = np.arange(1024, dtype=np.float64)
    ang = 2.0 * np.pi * np.outer(n, n) / 1024.0
    Fr = np.cos(ang).astype(np.float32)
    Fi = (-np.sin(ang)).astype(np.float32)
    Gr = (np.cos(ang) / 1024.0).astype(np.float32)
    Gi = (np.sin(ang) / 1024.0).astype(np.float32)
    return Fr, Fi, Gr, Gi


def rope_consts():
    inv = 1.0 / 10000.0 ** (np.arange(0, 64, 2, dtype=np.float32) / 64)
    freqs = np.arange(4096, dtype=np.float32)[:, None] * inv[None, :]
    return np.ascontiguousarray(np.cos(freqs).T.astype(np.float32)), \
           np.ascontiguousarray(np.sin(freqs).T.astype(np.float32))


def host_L1(x_fm, W, n_cores=8):
    ident = np.eye(128, dtype=np.float32)
    Fr, Fi, Gr, Gi = dft_consts()
    cosT, sinT = rope_consts()
    in_maps = []
    for core in range(n_cores):
        b, half = core // 2, core % 2
        in_maps.append(dict(
            xq=np.ascontiguousarray(x_fm[b][:, half * 2048 : (half + 1) * 2048]),
            xf=np.ascontiguousarray(x_fm[b]),
            wq=W["L1_attn_Wq"], wk=W["L1_attn_Wk"], wv=W["L1_attn_Wv"], wo=W["L1_attn_Wo"],
            cosT=cosT, sinT=sinT,
            coshalf=np.ascontiguousarray(cosT[:, half * 2048 : (half + 1) * 2048]),
            sinhalf=np.ascontiguousarray(sinT[:, half * 2048 : (half + 1) * 2048]),
            Fr=Fr, Fi=Fi, Gr=Gr, Gi=Gi,
            wfr=W["L1_ffn_w_freq_real"], wfi=W["L1_ffn_w_freq_imag"],
            fb=W["L1_ffn_bias"].reshape(1, -1),
            ident=ident,
        ))
    nc, P = build_L1()
    res = run_bass_kernel_spmd(nc, in_maps, core_ids=list(range(n_cores)))
    P.close()
    out = np.empty_like(x_fm)
    for core in range(n_cores):
        b, half = core // 2, core % 2
        out[b][:, half * 2048 : (half + 1) * 2048] = res.results[core]["xo"]
    return out




TW = 2051


def build_L2pre():
    nc = bass.Bass("TRN2", target_bir_lowering=False)
    d_x = nc.dram_tensor("x", [1024, TW], F32, kind="ExternalInput").ap()
    d_lngb = nc.dram_tensor("lngb", [2, 1024], F32, kind="ExternalInput").ap()
    d_win = nc.dram_tensor("w_in", [1024, 4096], F32, kind="ExternalInput").ap()
    d_cw = nc.dram_tensor("conv_w", [2048, 4], F32, kind="ExternalInput").ap()
    d_cb = nc.dram_tensor("conv_b", [1, 2048], F32, kind="ExternalInput").ap()
    d_wxp = nc.dram_tensor("w_xp", [2048, 96], F32, kind="ExternalInput").ap()
    d_wdt = nc.dram_tensor("w_dt", [64, 2048], F32, kind="ExternalInput").ap()
    d_dtb = nc.dram_tensor("dt_bias", [1, 2048], F32, kind="ExternalInput").ap()
    d_ident = nc.dram_tensor("ident", [128, 128], F32, kind="ExternalInput").ap()
    o_xc = nc.dram_tensor("o_xc", [2048, 2048], BF16, kind="ExternalOutput").ap()
    o_z = nc.dram_tensor("o_z", [2048, 2048], BF16, kind="ExternalOutput").ap()
    o_delta = nc.dram_tensor("o_delta", [2048, 2048], F32, kind="ExternalOutput").ap()
    o_bc = nc.dram_tensor("o_bc", [32, 2048], F32, kind="ExternalOutput").ap()

    P = Prog(nc)
    em = Em(P)
    em.wstg_width = 1024
    em.no_gb01 = True
    em.setup_consts(d_ident, None)
    pb = em.pb
    gbw = P.sb("gbw", [2, 1024], F32)
    P.emit("sp", lambda e: e.dma_start(out=gbw[:, :], in_=d_lngb), writes=["gbw"], dma=True)
    prow = P.sb("prow", [1, 128], F32)
    P.emit("dve", lambda e: e.memset(prow[:, :], 1.0), writes=["prow"])

    xstg = [P.sb(f"xstg{i}", [128, 512], F32) for i in range(2)]
    prestg = [P.sb(f"prestg{i}", [128, 512], F32) for i in range(2)]
    xstg_i = [0]
    prestg_i = [0]

    def xload(ct, c0, cw):
        i = xstg_i[0]
        xstg_i[0] = (i + 1) % 2
        t = xstg[i]
        P.emit("sp", lambda e, t=t, ct=ct, c0=c0, cw=cw: e.dma_start(
            out=t[:, 0:cw], in_=d_x[ct * 128 : (ct + 1) * 128, c0 : c0 + cw]),
            writes=[t.name], dma=True)
        return t[:, 0:cw], t.name

    st_m = P.sb("st_m", [1, TW], F32)
    st_r = P.sb("st_r", [1, TW], F32)
    em.ln(None, None, TW, None, None, gb=gbw, gbkey="gbw", loader=xload,
          stats_rows=(st_m, st_r), stats_key="st_rows", apply=False)

    def preload(ct, c0, cw):
        raw, rkey = xload(ct, c0, cw)
        i = prestg_i[0]
        prestg_i[0] = (i + 1) % 2
        t = prestg[i]
        pA, pB = pb[2], pb[3]
        P.emit("te", lambda e, c0=c0, cw=cw, pA=pA: e.matmul(
            pA[:, 0:cw], prow[0:1, :], st_r[0:1, c0 : c0 + cw], start=True, stop=True),
            reads=["prow", "st_rows"], writes=[pA.name])
        P.emit("te", lambda e, c0=c0, cw=cw, pB=pB: e.matmul(
            pB[:, 0:cw], prow[0:1, :], st_m[0:1, c0 : c0 + cw], start=True, stop=True),
            reads=["prow", "st_rows"], writes=[pB.name])
        P.emit("dve", lambda e, t=t, raw=raw, cw=cw, pA=pA: e.tensor_tensor(
            t[:, 0:cw], raw, pA[:, 0:cw], ALU.mult), reads=[rkey, pA.name], writes=[t.name])
        P.emit("dve", lambda e, t=t, cw=cw, pB=pB: e.tensor_tensor(
            t[:, 0:cw], t[:, 0:cw], pB[:, 0:cw], ALU.add), reads=[t.name, pB.name], writes=[t.name])
        return t[:, 0:cw], t.name

    import os as _os
    if _os.environ.get("L2DBG") == "xp":
        xpstg = P.sb("xpstg", [128, 512], F32)
        for ct in range(8):
            for c0 in range(0, 2048, 512):
                ap, k = preload(ct, c0 + 3, 512)
                P.emit("dve", lambda e, ap=ap: e.tensor_copy(xpstg[:, :], ap), reads=[k], writes=["xpstg"])
                P.emit("pool", lambda e, ct=ct, c0=c0: e.dma_start(
                    out=o_xc[ct * 128 : (ct + 1) * 128, c0 : c0 + 512], in_=xpstg[:, :]),
                    reads=["xpstg"], writes=[], dma=True)
        P.finalize()
        return nc, P

    xn = [P.sb(f"xn{i}", [128, TW], BF16) for i in range(8)]
    xnk = [f"xn{i}" for i in range(8)]
    em.ln(None, None, TW, xn, xnk, gb=gbw, gbkey="gbw", loader=preload)

    import os
    if os.environ.get("L2DBG") == "xn":
        for ct in range(8):
            P.emit("pool", lambda e, ct=ct: e.dma_start(
                out=o_xc[ct * 128 : (ct + 1) * 128, :], in_=xn[ct][:, 3 : 3 + 2048]),
                reads=[xnk[ct]], writes=[], dma=True)
        P.finalize()
        return nc, P

    cw_pc = P.sb("cw_pc", [128, 64], F32)
    fns = []
    for dt16 in range(16):
        fns.append(lambda e, dt16=dt16: e.dma_start(
            out=cw_pc[:, dt16 * 4 : (dt16 + 1) * 4], in_=d_cw[dt16 * 128 : (dt16 + 1) * 128, :]))
    P.emit("pool", fns, writes=["cw_pc"], dma=True)
    cb_pc, _ = em.load_pcol(d_cb.rearrange("a b -> (a b)"), 2048, key="cb_pc")
    dtb_pc, _ = em.load_pcol(d_dtb.rearrange("a b -> (a b)"), 2048, key="dtb_pc")

    winsl = [P.sb(f"winsl{i}", [128, 128], BF16) for i in range(16)]
    xiT = P.sb("xiT", [128, TW], BF16)
    xcT = [P.sb(f"xcT{i}", [128, 2048], BF16) for i in range(16)]
    xcTk = [f"xcT{i}" for i in range(16)]
    cstg = P.sb("cstg", [128, 512], F32)
    zstg = P.sb("zstg", [128, 512], BF16)

    CH = [(0, 512), (512, 512), (1024, 512), (1536, 512), (2048, 3)]
    for ot in range(32):
        par = ot % 2
        wk = f"winsl{par}"
        P.emit("pool", [(lambda e, k=k, ot=ot, par=par: e.dma_start(
            out=winsl[par * 8 + k][:, :],
            in_=d_win[k * 128 : (k + 1) * 128, ot * 128 : (ot + 1) * 128])) for k in range(8)],
            writes=[wk], dma=True)
        if ot < 16:
            for c0, cwid in CH:
                pbank = pb[par]
                P.emit("te", [(lambda e, k=k, c0=c0, cwid=cwid, par=par, pbank=pbank: e.matmul(
                    pbank[:, 0:cwid], winsl[par * 8 + k][:, :], xn[k][:, c0 : c0 + cwid],
                    start=(k == 0), stop=(k == 7))) for k in range(8)],
                    reads=[wk] + xnk, writes=[pbank.name])
                P.emit("dve", lambda e, c0=c0, cwid=cwid, pbank=pbank: e.tensor_copy(
                    xiT[:, c0 : c0 + cwid], pbank[:, 0:cwid]), reads=[pbank.name], writes=["xiT"])
            for c0, cwid in [(0, 512), (512, 512), (1024, 512), (1536, 512)]:
                P.emit("dve", lambda e, ot=ot, c0=c0, cwid=cwid: e.tensor_scalar(
                    cstg[:, 0:cwid], xiT[:, c0 : c0 + cwid], cw_pc[:, ot * 4 : ot * 4 + 1],
                    None, ALU.mult), reads=["xiT", "cw_pc"], writes=["cstg"])
                for kk in range(1, 4):
                    P.emit("dve", lambda e, ot=ot, c0=c0, cwid=cwid, kk=kk: e.scalar_tensor_tensor(
                        cstg[:, 0:cwid], xiT[:, c0 + kk : c0 + kk + cwid],
                        cw_pc[:, ot * 4 + kk : ot * 4 + kk + 1], cstg[:, 0:cwid],
                        ALU.mult, ALU.add), reads=["xiT", "cw_pc", "cstg"], writes=["cstg"])
                P.emit("act", lambda e, ot=ot, c0=c0, cwid=cwid: e.activation(
                    xcT[ot][:, c0 : c0 + cwid], cstg[:, 0:cwid], AF.Silu,
                    bias=cb_pc[:, ot : ot + 1]), reads=["cstg", "cb_pc"], writes=[xcTk[ot]])
            P.emit("sp", lambda e, ot=ot: e.dma_start(
                out=o_xc[ot * 128 : (ot + 1) * 128, :], in_=xcT[ot][:, :]),
                reads=[xcTk[ot]], writes=[], dma=True)
        else:
            zt = ot - 16
            for c0 in range(0, 2048, 512):
                pbank = pb[par]
                P.emit("te", [(lambda e, k=k, c0=c0, par=par, pbank=pbank: e.matmul(
                    pbank[:, 0:512], winsl[par * 8 + k][:, :], xn[k][:, 3 + c0 : 3 + c0 + 512],
                    start=(k == 0), stop=(k == 7))) for k in range(8)],
                    reads=[wk] + xnk, writes=[pbank.name])
                P.emit("dve", lambda e, c0=c0, pbank=pbank: e.tensor_copy(
                    zstg[:, :], pbank[:, 0:512]), reads=[pbank.name], writes=["zstg"])
                P.emit("sp", lambda e, zt=zt, c0=c0: e.dma_start(
                    out=o_z[zt * 128 : (zt + 1) * 128, c0 : c0 + 512], in_=zstg[:, :]),
                    reads=["zstg"], writes=[], dma=True)

    wxpsl = [P.sb(f"wxpsl{i}", [128, 96], BF16) for i in range(16)]
    P.emit("pool", [(lambda e, k=k: e.dma_start(
        out=wxpsl[k][:, :], in_=d_wxp[k * 128 : (k + 1) * 128, :])) for k in range(16)],
        writes=["wxpsl"], dma=True)
    xdblT = P.sb("xdblT", [96, 2048], BF16)
    bcstg = P.sb("bcstg", [32, 2048], F32)
    for c0 in range(0, 2048, 512):
        pbank = pb[(c0 // 512) % 2]
        P.emit("te", [(lambda e, k=k, c0=c0, pbank=pbank: e.matmul(
            pbank[0:96, 0:512], wxpsl[k][:, :], xcT[k][:, c0 : c0 + 512],
            start=(k == 0), stop=(k == 15))) for k in range(16)],
            reads=["wxpsl"] + xcTk, writes=[pbank.name])
        P.emit("dve", lambda e, c0=c0, pbank=pbank: e.tensor_copy(
            xdblT[:, c0 : c0 + 512], pbank[0:96, 0:512]), reads=[pbank.name], writes=["xdblT"])
        P.emit("dve", lambda e, c0=c0, pbank=pbank: e.tensor_copy(
            bcstg[:, c0 : c0 + 512], pbank[64:96, 0:512]), reads=[pbank.name], writes=["bcstg"])
    P.emit("sp", lambda e: e.dma_start(out=o_bc, in_=bcstg[:, :]), reads=["bcstg"], writes=[], dma=True)

    wdtsl = P.sb("wdtsl", [64, 2048], BF16)
    P.emit("pool", lambda e: e.dma_start(out=wdtsl[:, :], in_=d_wdt), writes=["wdtsl"], dma=True)
    dstg = [P.sb(f"dstg{i}", [128, 512], F32) for i in range(3)]
    for dt16 in range(16):
        for c0 in range(0, 2048, 512):
            pbank = pb[(c0 // 512) % 2]
            P.emit("te", lambda e, dt16=dt16, c0=c0, pbank=pbank: e.matmul(
                pbank[:, 0:512], wdtsl[:, dt16 * 128 : (dt16 + 1) * 128],
                xdblT[0:64, c0 : c0 + 512], start=True, stop=True),
                reads=["wdtsl", "xdblT"], writes=[pbank.name])
            P.emit("dve", lambda e, dt16=dt16, pbank=pbank: e.tensor_scalar(
                dstg[0][:, :], pbank[:, 0:512], dtb_pc[:, dt16 : dt16 + 1], None, ALU.add),
                reads=[pbank.name, "dtb_pc"], writes=[dstg[0].name])
            P.emit("dve", lambda e: e.tensor_scalar(dstg[1][:, :], dstg[0][:, :], -1.0, None, ALU.mult),
                   reads=[dstg[0].name], writes=[dstg[1].name])
            P.emit("dve", lambda e: e.tensor_tensor(dstg[1][:, :], dstg[0][:, :], dstg[1][:, :], ALU.max),
                   reads=[dstg[0].name, dstg[1].name], writes=[dstg[1].name])
            P.emit("act", lambda e: e.activation(dstg[2][:, :], dstg[1][:, :], AF.Exp, scale=-1.0),
                   reads=[dstg[1].name], writes=[dstg[2].name])
            P.emit("act", lambda e: e.activation(dstg[2][:, :], dstg[2][:, :], AF.Ln,
                   bias=em.ones_col[:, :]), reads=[dstg[2].name, "ones_col"], writes=[dstg[2].name])
            P.emit("dve", lambda e: e.tensor_scalar(dstg[0][:, :], dstg[0][:, :], 0.0, None, ALU.max),
                   reads=[dstg[0].name], writes=[dstg[0].name])
            P.emit("dve", lambda e: e.tensor_tensor(dstg[0][:, :], dstg[0][:, :], dstg[2][:, :], ALU.add),
                   reads=[dstg[0].name, dstg[2].name], writes=[dstg[0].name])
            P.emit("sp", lambda e, dt16=dt16, c0=c0: e.dma_start(
                out=o_delta[dt16 * 128 : (dt16 + 1) * 128, c0 : c0 + 512], in_=dstg[0][:, :]),
                reads=[dstg[0].name], writes=[], dma=True)

    P.finalize()
    return nc, P




def build_L2scan():
    nc = bass.Bass("TRN2", target_bir_lowering=False)
    d_delta = nc.dram_tensor("delta", [1024, 4096], F32, kind="ExternalInput").ap()
    d_xc = nc.dram_tensor("xc", [1024, 4096], BF16, kind="ExternalInput").ap()
    d_B = nc.dram_tensor("Brow", [16, 4096], F32, kind="ExternalInput").ap()
    d_C = nc.dram_tensor("Crow", [16, 4096], F32, kind="ExternalInput").ap()
    d_alog = nc.dram_tensor("alog_pc", [128, 128], F32, kind="ExternalInput").ap()
    d_ident = nc.dram_tensor("ident", [128, 128], F32, kind="ExternalInput").ap()
    o_y = nc.dram_tensor("o_y", [1024, 4096], BF16, kind="ExternalOutput").ap()

    P = Prog(nc)
    em = Em(P)
    em.wstg_width = 128
    em.no_gb01 = True
    em.setup_consts(d_ident, None)
    pb = em.pb
    ones128 = P.sb("ones128", [1, 128], F32)
    P.emit("dve", lambda e: e.memset(ones128[:, :], 1.0), writes=["ones128"])
    apc = P.sb("apc", [128, 128], F32)
    P.emit("sp", lambda e: e.dma_start(out=apc[:, :], in_=d_alog), writes=["apc"], dma=True)
    P.emit("act", lambda e: e.activation(apc[:, :], apc[:, :], AF.Exp), reads=["apc"], writes=["apc"])
    P.emit("dve", lambda e: e.tensor_scalar(apc[:, :], apc[:, :], -1.0, None, ALU.mult),
           reads=["apc"], writes=["apc"])
    brow = P.sb("brow", [1, 4096], F32)
    crow = P.sb("crow", [1, 4096], F32)

    dti = [P.sb(f"dti{i}", [128, 4096], F32) for i in range(2)]
    xci = [P.sb(f"xci{i}", [128, 4096], BF16) for i in range(2)]
    du = P.sb("du", [128, 4096], F32)
    dA = P.sb("dA", [128, 4096], F32)
    bu = P.sb("bu", [128, 4096], F32)
    ht = P.sb("ht", [128, 4096], F32)
    yac = P.sb("yac", [128, 4096], F32)
    tmp = P.sb("tmp", [128, 512], F32)

    for dt8 in range(8):
        par = dt8 % 2
        P.emit("sp", lambda e, dt8=dt8, par=par: e.dma_start(
            out=dti[par][:, :], in_=d_delta[dt8 * 128 : (dt8 + 1) * 128, :]),
            writes=[dti[par].name], dma=True)
        P.emit("sp", lambda e, dt8=dt8, par=par: e.dma_start(
            out=xci[par][:, :], in_=d_xc[dt8 * 128 : (dt8 + 1) * 128, :]),
            writes=[xci[par].name], dma=True)
        P.emit("dve", lambda e, par=par: e.tensor_tensor(du[:, :], dti[par][:, :], xci[par][:, :],
               ALU.mult), reads=[dti[par].name, xci[par].name], writes=["du"])
        for n in range(16):
            col = dt8 * 16 + n
            P.emit("sp", lambda e, n=n: e.dma_start(out=brow[:, :], in_=d_B[n : n + 1, :]),
                   writes=["brow"], dma=True)
            P.emit("sp", lambda e, n=n: e.dma_start(out=crow[:, :], in_=d_C[n : n + 1, :]),
                   writes=["crow"], dma=True)
            P.emit("act", lambda e, par=par, col=col: e.activation(
                dA[:, :], dti[par][:, :], AF.Exp, scale=apc[:, col : col + 1]),
                reads=[dti[par].name, "apc"], writes=["dA"])
            for c0 in range(0, 4096, 512):
                pbank = pb[(c0 // 512) % 2]
                P.emit("te", lambda e, c0=c0, pbank=pbank: e.matmul(
                    pbank[:, 0:512], ones128[:, :], brow[0:1, c0 : c0 + 512],
                    start=True, stop=True), reads=["ones128", "brow"], writes=[pbank.name])
                P.emit("dve", lambda e, c0=c0, pbank=pbank: e.tensor_tensor(
                    bu[:, c0 : c0 + 512], du[:, c0 : c0 + 512], pbank[:, 0:512], ALU.mult),
                    reads=["du", pbank.name], writes=["bu"])
            P.emit("dve", lambda e: e.tensor_tensor_scan(
                ht[:, :], dA[:, :], bu[:, :], 0.0, ALU.mult, ALU.add),
                reads=["dA", "bu"], writes=["ht"])
            for c0 in range(0, 4096, 512):
                pbank = pb[2 + (c0 // 512) % 2]
                P.emit("te", lambda e, c0=c0, pbank=pbank: e.matmul(
                    pbank[:, 0:512], ones128[:, :], crow[0:1, c0 : c0 + 512],
                    start=True, stop=True), reads=["ones128", "crow"], writes=[pbank.name])
                if n == 0:
                    P.emit("dve", lambda e, c0=c0, pbank=pbank: e.tensor_tensor(
                        yac[:, c0 : c0 + 512], ht[:, c0 : c0 + 512], pbank[:, 0:512], ALU.mult),
                        reads=["ht", pbank.name], writes=["yac"])
                else:
                    P.emit("dve", lambda e, c0=c0, pbank=pbank: e.tensor_tensor(
                        tmp[:, :], ht[:, c0 : c0 + 512], pbank[:, 0:512], ALU.mult),
                        reads=["ht", pbank.name], writes=["tmp"])
                    P.emit("dve", lambda e, c0=c0: e.tensor_tensor(
                        yac[:, c0 : c0 + 512], yac[:, c0 : c0 + 512], tmp[:, :], ALU.add),
                        reads=["yac", "tmp"], writes=["yac"])
        P.emit("pool", lambda e, dt8=dt8: e.dma_start(
            out=o_y[dt8 * 128 : (dt8 + 1) * 128, :], in_=yac[:, :]),
            reads=["yac"], writes=[], dma=True)

    P.finalize()
    return nc, P




def build_L2post():
    nc = bass.Bass("TRN2", target_bir_lowering=False)
    d_x = nc.dram_tensor("x", [1024, 2048], F32, kind="ExternalInput").ap()
    d_y = nc.dram_tensor("y", [2048, 2048], BF16, kind="ExternalInput").ap()
    d_xc = nc.dram_tensor("xc", [2048, 2048], BF16, kind="ExternalInput").ap()
    d_z = nc.dram_tensor("z", [2048, 2048], BF16, kind="ExternalInput").ap()
    d_dsk = nc.dram_tensor("dskip", [1, 2048], F32, kind="ExternalInput").ap()
    d_wout = nc.dram_tensor("w_out", [2048, 1024], F32, kind="ExternalInput").ap()
    d_ident = nc.dram_tensor("ident", [128, 128], F32, kind="ExternalInput").ap()
    d_xo = nc.dram_tensor("xo", [1024, 2048], F32, kind="ExternalOutput").ap()

    P = Prog(nc)
    em = Em(P)
    em.wstg_width = 128
    em.no_gb01 = True
    em.setup_consts(d_ident, None)
    pb = em.pb
    dsk_pc, _ = em.load_pcol(d_dsk.rearrange("a b -> (a b)"), 2048, key="dsk_pc")
    gbw = P.sb("gbw", [2, 1024], F32)
    P.emit("dve", lambda e: e.memset(gbw[0:2, :], 0.0), writes=["gbw"])
    P.emit("dve", lambda e: e.memset(gbw[0:1, :], 1.0), writes=["gbw"])

    xres = [P.sb(f"xres{i}", [128, 2048], F32) for i in range(8)]
    xresk = [f"xres{i}" for i in range(8)]
    x0t = [P.sb(f"x0t{i}", [128, 2048], BF16) for i in range(8)]
    x0k = [f"x0t{i}" for i in range(8)]
    P.emit("sp", [(lambda e, ct=ct: e.dma_start(
        out=xres[ct][:, :], in_=d_x[ct * 128 : (ct + 1) * 128, :])) for ct in range(8)],
        writes=xresk, dma=True)
    em.ln(xres, xresk, 2048, x0t, x0k, gb=gbw, gbkey="gbw")
    for ct in range(8):
        P.emit("dve", lambda e, ct=ct: e.tensor_tensor(
            xres[ct][:, :], xres[ct][:, :], x0t[ct][:, :], ALU.add),
            reads=[xresk[ct], x0k[ct]], writes=[xresk[ct]])

    gT = [P.sb(f"gT{i}", [128, 2048], BF16) for i in range(16)]
    gTk = [f"gT{i}" for i in range(16)]
    ystg = P.sb("ystg", [128, 2048], BF16)
    xcstg = P.sb("xcstg", [128, 2048], BF16)
    zstg = P.sb("zstg", [128, 2048], BF16)
    t1 = P.sb("t1", [128, 2048], BF16)
    for dt16 in range(16):
        P.emit("sp", lambda e, dt16=dt16: e.dma_start(
            out=ystg[:, :], in_=d_y[dt16 * 128 : (dt16 + 1) * 128, :]), writes=["ystg"], dma=True)
        P.emit("sp", lambda e, dt16=dt16: e.dma_start(
            out=xcstg[:, :], in_=d_xc[dt16 * 128 : (dt16 + 1) * 128, :]), writes=["xcstg"], dma=True)
        P.emit("sp", lambda e, dt16=dt16: e.dma_start(
            out=zstg[:, :], in_=d_z[dt16 * 128 : (dt16 + 1) * 128, :]), writes=["zstg"], dma=True)
        P.emit("act", lambda e: e.activation(zstg[:, :], zstg[:, :], AF.Silu),
               reads=["zstg"], writes=["zstg"])
        P.emit("dve", lambda e, dt16=dt16: e.scalar_tensor_tensor(
            t1[:, :], xcstg[:, :], dsk_pc[:, dt16 : dt16 + 1], ystg[:, :], ALU.mult, ALU.add),
            reads=["xcstg", "dsk_pc", "ystg"], writes=["t1"])
        P.emit("dve", lambda e, dt16=dt16: e.tensor_tensor(
            gT[dt16][:, :], t1[:, :], zstg[:, :], ALU.mult),
            reads=["t1", "zstg"], writes=[gTk[dt16]])

    woutsl = [P.sb(f"woutsl{i}", [128, 128], BF16) for i in range(32)]
    ostg = P.sb("ostg", [128, 512], F32)
    for ot in range(8):
        par = ot % 2
        wk = f"wosl{par}"
        P.emit("pool", [(lambda e, k=k, ot=ot, par=par: e.dma_start(
            out=woutsl[par * 16 + k][:, :],
            in_=d_wout[k * 128 : (k + 1) * 128, ot * 128 : (ot + 1) * 128])) for k in range(16)],
            writes=[wk], dma=True)
        for c0 in range(0, 2048, 512):
            pbank = pb[par]
            P.emit("te", [(lambda e, k=k, c0=c0, par=par, pbank=pbank: e.matmul(
                pbank[:, 0:512], woutsl[par * 16 + k][:, :], gT[k][:, c0 : c0 + 512],
                start=(k == 0), stop=(k == 15))) for k in range(16)],
                reads=[wk] + gTk, writes=[pbank.name])
            P.emit("dve", lambda e, ot=ot, c0=c0, pbank=pbank: e.tensor_tensor(
                ostg[:, :], pbank[:, 0:512], xres[ot][:, c0 : c0 + 512], ALU.add),
                reads=[pbank.name, xresk[ot]], writes=["ostg"])
            P.emit("sp", lambda e, ot=ot, c0=c0: e.dma_start(
                out=d_xo[ot * 128 : (ot + 1) * 128, c0 : c0 + 512], in_=ostg[:, :]),
                reads=["ostg"], writes=[], dma=True)

    P.finalize()
    return nc, P


def host_L2(x_fm, W, n_cores=8):
    ident = np.eye(128, dtype=np.float32)
    in_maps = []
    for core in range(n_cores):
        b, half = core // 2, core % 2
        lo = half * 2048 - 3
        xh = np.zeros((1024, TW), np.float32)
        s = max(lo, 0)
        xh[:, s - lo :] = x_fm[b][:, s : half * 2048 + 2048]
        in_maps.append(dict(
            x=np.ascontiguousarray(xh),
            lngb=np.stack([W["L2_attn_ln_g"], W["L2_attn_ln_b"]]).astype(np.float32),
            w_in=W["L2_attn_W_in"], conv_w=np.ascontiguousarray(W["L2_attn_conv_w"][:, 0, :]),
            conv_b=W["L2_attn_conv_b"].reshape(1, -1), w_xp=W["L2_attn_W_xp"],
            w_dt=W["L2_attn_W_dt"], dt_bias=W["L2_attn_dt_bias"].reshape(1, -1),
            ident=ident))
    nc, P = build_L2pre()
    res = run_bass_kernel_spmd(nc, in_maps, core_ids=list(range(n_cores)))
    P.close()
    pre = [res.results[c] for c in range(n_cores)]

    nb = n_cores // 2
    alog = W["L2_attn_A_log"]
    in_maps = []
    for core in range(n_cores):
        b, dh = core // 2, core % 2
        dsl = slice(dh * 1024, (dh + 1) * 1024)
        delta = np.concatenate([pre[2 * b]["o_delta"][dsl], pre[2 * b + 1]["o_delta"][dsl]], axis=1)
        xc = np.concatenate([pre[2 * b]["o_xc"][dsl], pre[2 * b + 1]["o_xc"][dsl]], axis=1)
        Brow = np.concatenate([pre[2 * b]["o_bc"][0:16], pre[2 * b + 1]["o_bc"][0:16]], axis=1)
        Crow = np.concatenate([pre[2 * b]["o_bc"][16:32], pre[2 * b + 1]["o_bc"][16:32]], axis=1)
        al = alog[dsl.start : dsl.stop].reshape(8, 128, 16).transpose(1, 0, 2).reshape(128, 128)
        in_maps.append(dict(delta=np.ascontiguousarray(delta), xc=np.ascontiguousarray(xc),
                            Brow=np.ascontiguousarray(Brow), Crow=np.ascontiguousarray(Crow),
                            alog_pc=np.ascontiguousarray(al.astype(np.float32)), ident=ident))
    nc, P = build_L2scan()
    res = run_bass_kernel_spmd(nc, in_maps, core_ids=list(range(n_cores)))
    P.close()
    ys = [res.results[c]["o_y"] for c in range(n_cores)]

    in_maps = []
    for core in range(n_cores):
        b, half = core // 2, core % 2
        tsl = slice(half * 2048, (half + 1) * 2048)
        y = np.concatenate([ys[2 * b][:, tsl], ys[2 * b + 1][:, tsl]], axis=0)
        in_maps.append(dict(
            x=np.ascontiguousarray(x_fm[b][:, tsl]), y=np.ascontiguousarray(y),
            xc=pre[core]["o_xc"], z=pre[core]["o_z"],
            dskip=W["L2_attn_D_skip"].reshape(1, -1), w_out=W["L2_attn_W_out"], ident=ident))
    nc, P = build_L2post()
    res = run_bass_kernel_spmd(nc, in_maps, core_ids=list(range(n_cores)))
    P.close()
    out = np.empty_like(x_fm)
    for core in range(n_cores):
        b, half = core // 2, core % 2
        out[b][:, half * 2048 : (half + 1) * 2048] = res.results[core]["xo"]
    return out





def build_L2f():
    nc = bass.Bass("TRN2", target_bir_lowering=False)
    d_x = nc.dram_tensor("x", [1024, 2048], F32, kind="ExternalInput").ap()
    dW = {}
    for w in ["wf", "uf", "wi", "ui", "wc", "uc", "wo", "uo"]:
        dW[w] = nc.dram_tensor(w, [1024, 1024], F32, kind="ExternalInput").ap()
    d_bf = nc.dram_tensor("bf", [1, 1024], F32, kind="ExternalInput").ap()
    d_bi = nc.dram_tensor("bi", [1, 1024], F32, kind="ExternalInput").ap()
    d_bc = nc.dram_tensor("bc", [1, 1024], F32, kind="ExternalInput").ap()
    d_bo = nc.dram_tensor("bo", [1, 1024], F32, kind="ExternalInput").ap()
    d_inc = nc.dram_tensor("inc", [1, 1024], F32, kind="ExternalInput").ap()
    d_ident = nc.dram_tensor("ident", [128, 128], F32, kind="ExternalInput").ap()
    d_xo = nc.dram_tensor("xo", [1024, 2048], F32, kind="ExternalOutput").ap()

    P = Prog(nc)
    em = Em(P)
    em.wstg_width = 1024
    em.no_gb01 = True
    em.setup_consts(d_ident, None)
    pb = em.pb
    gbw = P.sb("gbw", [2, 1024], F32)
    P.emit("dve", lambda e: e.memset(gbw[0:2, :], 0.0), writes=["gbw"])
    P.emit("dve", lambda e: e.memset(gbw[0:1, :], 1.0), writes=["gbw"])
    ones512f = P.sb("ones512f", [1, 512], F32)
    P.emit("dve", lambda e: e.memset(ones512f[:, :], 1.0), writes=["ones512f"])
    inc_row = P.sb("inc_row", [1, 1024], F32)
    P.emit("sp", lambda e: e.dma_start(out=inc_row[:, :], in_=d_inc), writes=["inc_row"], dma=True)
    bpc = {}
    for nm, d in [("bf", d_bf), ("bi", d_bi), ("bc", d_bc), ("bo", d_bo)]:
        bpc[nm], _ = em.load_pcol(d.rearrange("a b -> (a b)"), 1024, key=f"pc_{nm}")

    wT = {}
    for w in ["wf", "uf", "wi", "ui", "wc", "uc"]:
        wT[w] = em.wT(dW[w], 1024, 1024, dst_key=f"wT_{w}")[0]

    xstg = [P.sb(f"xstg{i}", [128, 512], F32) for i in range(2)]
    xstg_i = [0]

    def xload(ct, c0, cw, base=0):
        i = xstg_i[0]
        xstg_i[0] = (i + 1) % 2
        t = xstg[i]
        P.emit("sp", lambda e, t=t, ct=ct, c0=c0, cw=cw, base=base: e.dma_start(
            out=t[:, 0:cw], in_=d_x[ct * 128 : (ct + 1) * 128, base + c0 : base + c0 + cw]),
            writes=[t.name], dma=True)
        return t[:, 0:cw], t.name

    ct_f = [P.sb(f"cst{i}", [128, 512], F32) for i in range(8)]
    cbf = [P.sb(f"cbf{i}", [128, 512], BF16) for i in range(8)]
    lnt = [P.sb(f"lnt{i}", [128, 512], BF16) for i in range(8)]
    lntk = [f"lnt{i}" for i in range(8)]
    gf = [P.sb(f"gf{i}", [128, 512], BF16) for i in range(8)]
    gi = [P.sb(f"gi{i}", [128, 512], BF16) for i in range(8)]
    gc = [P.sb(f"gc{i}", [128, 512], BF16) for i in range(8)]
    tmp = P.sb("ltmp", [128, 512], F32)

    for ctl in range(8):
        P.emit("te", lambda e, ctl=ctl: e.matmul(
            pb[0][:, 0:512], inc_row[0:1, ctl * 128 : (ctl + 1) * 128], ones512f[:, :],
            start=True, stop=True), reads=["inc_row", "ones512f"], writes=[pb[0].name])
        P.emit("dve", lambda e, ctl=ctl: e.tensor_copy(ct_f[ctl][:, :], pb[0][:, 0:512]),
               reads=[pb[0].name], writes=[ct_f[ctl].name])

    def gate(wname, uname, bname, act, dst, dstk):
        for ot in range(8):
            pbank = pb[ot % 2]
            fns = [(lambda e, k=k, ot=ot, pbank=pbank, wname=wname: e.matmul(
                pbank[:, 0:512], wT[wname][k][:, ot * 128 : (ot + 1) * 128], lnt[k][:, :],
                start=(k == 0), stop=False)) for k in range(8)]
            fns += [(lambda e, k=k, ot=ot, pbank=pbank, uname=uname: e.matmul(
                pbank[:, 0:512], wT[uname][k][:, ot * 128 : (ot + 1) * 128], cbf[k][:, :],
                start=False, stop=(k == 7))) for k in range(8)]
            P.emit("te", fns, reads=[f"wT_{wname}", f"wT_{uname}"] + lntk +
                   [cbf[k].name for k in range(8)], writes=[pbank.name])
            P.emit("act", lambda e, ot=ot, pbank=pbank, bname=bname, act=act, dst=dst: e.activation(
                dst[ot][:, :], pbank[:, 0:512], act, bias=bpc[bname][:, ot : ot + 1]),
                reads=[pbank.name, f"pc_{bname}"], writes=[dstk[ot]])

    for t in range(4):
        em.ln(None, None, 512, lnt, lntk, gb=gbw, gbkey="gbw",
              loader=lambda ctl, c0, cw, t=t: xload(ctl, c0, cw, base=t * 512))
        for ctl in range(8):
            P.emit("dve", lambda e, ctl=ctl: e.tensor_copy(cbf[ctl][:, :], ct_f[ctl][:, :]),
                   reads=[ct_f[ctl].name], writes=[cbf[ctl].name])
        gate("wf", "uf", "bf", AF.Sigmoid, gf, [gf[i].name for i in range(8)])
        gate("wi", "ui", "bi", AF.Sigmoid, gi, [gi[i].name for i in range(8)])
        gate("wc", "uc", "bc", AF.Tanh, gc, [gc[i].name for i in range(8)])
        for ctl in range(8):
            P.emit("dve", lambda e, ctl=ctl: e.tensor_tensor(
                ct_f[ctl][:, :], gf[ctl][:, :], ct_f[ctl][:, :], ALU.mult),
                reads=[gf[ctl].name, ct_f[ctl].name], writes=[ct_f[ctl].name])
            P.emit("dve", lambda e, ctl=ctl: e.tensor_tensor(
                tmp[:, :], gi[ctl][:, :], gc[ctl][:, :], ALU.mult),
                reads=[gi[ctl].name, gc[ctl].name], writes=["ltmp"])
            P.emit("dve", lambda e, ctl=ctl: e.tensor_tensor(
                ct_f[ctl][:, :], ct_f[ctl][:, :], tmp[:, :], ALU.add),
                reads=[ct_f[ctl].name, "ltmp"], writes=[ct_f[ctl].name])

    for ctl in range(8):
        P.emit("dve", lambda e, ctl=ctl: e.tensor_copy(cbf[ctl][:, :], ct_f[ctl][:, :]),
               reads=[ct_f[ctl].name], writes=[cbf[ctl].name])
    em.wT(dW["wo"], 1024, 1024, dst_tiles=wT["wf"], dst_key="wT_wf")
    em.wT(dW["uo"], 1024, 1024, dst_tiles=wT["uf"], dst_key="wT_uf")
    ostg = P.sb("ostg", [128, 512], F32)
    for t in range(4):
        em.ln(None, None, 512, lnt, lntk, gb=gbw, gbkey="gbw",
              loader=lambda ctl, c0, cw, t=t: xload(ctl, c0, cw, base=t * 512))
        for ot in range(8):
            pbank = pb[2 + ot % 2]
            fns = [(lambda e, k=k, ot=ot, pbank=pbank: e.matmul(
                pbank[:, 0:512], wT["wf"][k][:, ot * 128 : (ot + 1) * 128], lnt[k][:, :],
                start=(k == 0), stop=False)) for k in range(8)]
            fns += [(lambda e, k=k, ot=ot, pbank=pbank: e.matmul(
                pbank[:, 0:512], wT["uf"][k][:, ot * 128 : (ot + 1) * 128], cbf[k][:, :],
                start=False, stop=(k == 7))) for k in range(8)]
            P.emit("te", fns, reads=["wT_wf", "wT_uf"] + lntk +
                   [cbf[k].name for k in range(8)], writes=[pbank.name])
            P.emit("act", lambda e, ot=ot, pbank=pbank: e.activation(
                tmp[:, :], pbank[:, 0:512], AF.Tanh, bias=bpc["bo"][:, ot : ot + 1]),
                reads=[pbank.name, "pc_bo"], writes=["ltmp"])
            xap, xkey = xload(ot, t * 512, 512)
            P.emit("dve", lambda e, xap=xap: e.tensor_tensor(
                ostg[:, :], tmp[:, :], xap, ALU.add), reads=["ltmp", xkey], writes=["ostg"])
            P.emit("sp", lambda e, ot=ot, t=t: e.dma_start(
                out=d_xo[ot * 128 : (ot + 1) * 128, t * 512 : (t + 1) * 512], in_=ostg[:, :]),
                reads=["ostg"], writes=[], dma=True)

    P.finalize()
    return nc, P


def host_L2f(x_fm, W, n_cores=8):
    ident = np.eye(128, dtype=np.float32)
    in_maps = []
    for core in range(n_cores):
        js = slice(core * 512, (core + 1) * 512)
        xj = np.concatenate([x_fm[b][:, js] for b in range(4)], axis=1)
        m = dict(x=np.ascontiguousarray(xj), ident=ident,
                 bf=W["L2_ffn_bf"].reshape(1, -1), bi=W["L2_ffn_bi"].reshape(1, -1),
                 bc=W["L2_ffn_bc"].reshape(1, -1), bo=W["L2_ffn_bo"].reshape(1, -1),
                 inc=W["L2_ffn_in_c"].reshape(1, -1))
        for w in ["wf", "uf", "wi", "ui", "wc", "uc", "wo", "uo"]:
            m[w] = W[f"L2_ffn_{w}"]
        in_maps.append(m)
    nc, P = build_L2f()
    res = run_bass_kernel_spmd(nc, in_maps, core_ids=list(range(n_cores)))
    P.close()
    out = np.empty_like(x_fm)
    for core in range(n_cores):
        js = slice(core * 512, (core + 1) * 512)
        for b in range(4):
            out[b][:, js] = res.results[core]["xo"][:, b * 512 : (b + 1) * 512]
    return out





def build_L3a():
    nc = bass.Bass("TRN2", target_bir_lowering=False)
    d_xf = nc.dram_tensor("xf", [1024, 4096], F32, kind="ExternalInput").ap()
    d_wq = nc.dram_tensor("wq", [512, 1024], F32, kind="ExternalInput").ap()
    d_wk = nc.dram_tensor("wk", [512, 1024], F32, kind="ExternalInput").ap()
    d_wv = nc.dram_tensor("wv", [512, 1024], F32, kind="ExternalInput").ap()
    d_wo = nc.dram_tensor("wo", [1024, 512], F32, kind="ExternalInput").ap()
    d_bq = nc.dram_tensor("bq", [1, 512], F32, kind="ExternalInput").ap()
    d_bk = nc.dram_tensor("bk", [1, 512], F32, kind="ExternalInput").ap()
    d_bv = nc.dram_tensor("bv", [1, 512], F32, kind="ExternalInput").ap()
    d_rk = nc.dram_tensor("rel_k", [257, 64], F32, kind="ExternalInput").ap()
    d_rv = nc.dram_tensor("rel_v", [257, 64], F32, kind="ExternalInput").ap()
    d_mask = nc.dram_tensor("maskL", [512, 512], F32, kind="ExternalInput").ap()
    d_ident = nc.dram_tensor("ident", [128, 128], F32, kind="ExternalInput").ap()
    o_part = nc.dram_tensor("o_part", [1024, 4096], F32, kind="ExternalOutput").ap()
    s_kt = nc.dram_tensor("s_kt", [8, 64, 4096], BF16, kind="Internal").ap()
    s_v = nc.dram_tensor("s_v", [8, 4096, 65], BF16, kind="Internal").ap()
    s_proj = nc.dram_tensor("s_proj", [4096, 257], BF16, kind="Internal").ap()
    s_pd = nc.dram_tensor("s_pd", [768, 512], BF16, kind="Internal").ap()
    s_q = nc.dram_tensor("s_q", [512, 4096], BF16, kind="Internal").ap()

    P = Prog(nc)
    em = Em(P)
    em.wstg_width = 1024
    em.no_gb01 = True
    em.setup_consts(d_ident, None)
    pb = em.pb
    gbw = P.sb("gbw", [2, 1024], F32)
    P.emit("dve", lambda e: e.memset(gbw[0:2, :], 0.0), writes=["gbw"])
    P.emit("dve", lambda e: e.memset(gbw[0:1, :], 1.0), writes=["gbw"])

    wqT = em.wT(d_wq, 512, 1024, dst_key="wqT")[0]
    wkT = em.wT(d_wk, 512, 1024, dst_key="wkT")[0]
    wvT = em.wT(d_wv, 512, 1024, dst_key="wvT")[0]
    woT = em.wT(d_wo, 1024, 512, dst_key="woT")[0]

    rkstg = P.sb("rkstg", [128, 64], BF16)
    relkT = P.sb("relkT", [64, 257], BF16)
    relv = [P.sb(f"relv{i}", [128, 64], BF16) for i in range(2)]
    relv2 = P.sb("relv2", [1, 64], BF16)
    for i in range(2):
        P.emit("pool", lambda e, i=i: e.dma_start(out=rkstg[:, :], in_=d_rk[i * 128 : (i + 1) * 128, :]),
               writes=["rkstg"], dma=True)
        P.emit("te", lambda e: e.transpose(em.ptr[0][0:64, 0:128], rkstg[:, :], em.ident[:, :]),
               reads=["rkstg", "ident"], writes=["ptr0"])
        P.emit("dve", lambda e, i=i: e.tensor_copy(relkT[:, i * 128 : (i + 1) * 128],
               em.ptr[0][0:64, 0:128]), reads=["ptr0"], writes=["relkT"])
        P.emit("pool", lambda e, i=i: e.dma_start(out=relv[i][:, :], in_=d_rv[i * 128 : (i + 1) * 128, :]),
               writes=["relv"], dma=True)
    P.emit("dve", lambda e: e.memset(rkstg[:, :], 0.0), writes=["rkstg"])
    P.emit("pool", lambda e: e.dma_start(out=rkstg[0:1, :], in_=d_rk[256:257, :]), writes=["rkstg"], dma=True)
    P.emit("te", lambda e: e.transpose(em.ptr[0][0:64, 0:128], rkstg[:, :], em.ident[:, :]),
           reads=["rkstg", "ident"], writes=["ptr0"])
    P.emit("dve", lambda e: e.tensor_copy(relkT[:, 256:257], em.ptr[0][0:64, 0:1]),
           reads=["ptr0"], writes=["relkT"])
    P.emit("pool", lambda e: e.dma_start(out=relv2[:, :], in_=d_rv[256:257, :]), writes=["relv"], dma=True)
    relkT2c = P.sb("relkT2c", [64, 2], BF16)
    P.emit("dve", lambda e: e.tensor_copy(relkT2c[:, 0:1], relkT[:, 0:1]), reads=["relkT"], writes=["relkT2c"])
    P.emit("dve", lambda e: e.tensor_copy(relkT2c[:, 1:2], relkT[:, 256:257]), reads=["relkT"], writes=["relkT2c"])

    brow = P.sb("l3brow", [1, 512], BF16)
    P.emit("pool", lambda e: e.dma_start(out=brow[0:1, 0:512], in_=d_bv), writes=["l3brow"], dma=True)
    bq_pc, _ = em.load_pcol(d_bq.rearrange("a b -> (a b)"), 512, key="bq_pc")
    bk_pc, _ = em.load_pcol(d_bk.rearrange("a b -> (a b)"), 512, key="bk_pc")
    maskt = P.sb("maskt", [128, 2048], BF16)
    P.emit("pool", [(lambda e, i=i: e.dma_start(out=maskt[:, i * 512 : (i + 1) * 512],
           in_=d_mask[i * 128 : (i + 1) * 128, :])) for i in range(4)], writes=["maskt"], dma=True)

    lnc = [P.sb(f"l3lnc{i}", [128, 512], BF16) for i in range(8)]
    lnck = [f"l3lnc{i}" for i in range(8)]
    xstg = [P.sb(f"l3xstg{i}", [128, 512], F32) for i in range(2)]
    xstg_i = [0]

    def xload(ct, c0, cw):
        i = xstg_i[0]
        xstg_i[0] = (i + 1) % 2
        t = xstg[i]
        P.emit("sp", lambda e, t=t, ct=ct, c0=c0, cw=cw: e.dma_start(
            out=t[:, 0:cw], in_=d_xf[ct * 128 : (ct + 1) * 128, c0 : c0 + cw]),
            writes=[t.name], dma=True)
        return t[:, 0:cw], t.name

    kstg = P.sb("l3kstg", [64, 512], BF16)
    vstg = P.sb("l3vstg", [128, 65], BF16)
    P.emit("dve", lambda e: e.memset(vstg[:, :], 1.0), writes=["l3vstg"])
    for ch in range(8):
        c0 = ch * 512
        em.ln(None, None, 512, lnc, lnck, gb=gbw, gbkey="gbw",
              loader=lambda ct, cc0, cw, c0=c0: xload(ct, c0 + cc0, cw))
        for h in range(8):
            co = h * 64
            fns = [(lambda e, k=k, co=co: e.matmul(
                pb[2][0:64, 0:512], wkT[k][:, co : co + 64], lnc[k][:, :],
                start=(k == 0), stop=(k == 7))) for k in range(8)]
            P.emit("te", fns, reads=["wkT"] + lnck, writes=[pb[2].name])
            bcol = bk_pc[(h % 2) * 64 : (h % 2) * 64 + 64, h // 2 : h // 2 + 1]
            P.emit("dve", lambda e, bcol=bcol: e.tensor_scalar(kstg[:, :], pb[2][0:64, 0:512],
                   bcol, None, ALU.add), reads=[pb[2].name, "bk_pc"], writes=["l3kstg"])
            P.emit("sp", lambda e, h=h, c0=c0: e.dma_start(
                out=s_kt[h, :, c0 : c0 + 512], in_=kstg[:, :]), reads=["l3kstg"], writes=[f"skt{h}"], dma=True)
            fns = [(lambda e, k=k, co=co: e.matmul(
                pb[3][0:64, 0:512], wqT[k][:, co : co + 64], lnc[k][:, :],
                start=(k == 0), stop=(k == 7))) for k in range(8)]
            P.emit("te", fns, reads=["wqT"] + lnck, writes=[pb[3].name])
            qcol = bq_pc[(h % 2) * 64 : (h % 2) * 64 + 64, h // 2 : h // 2 + 1]
            P.emit("dve", lambda e, qcol=qcol: e.tensor_scalar(kstg[:, :], pb[3][0:64, 0:512],
                   qcol, 0.125, ALU.add, ALU.mult), reads=[pb[3].name, "bq_pc"], writes=["l3kstg"])
            P.emit("sp", lambda e, h=h, c0=c0: e.dma_start(
                out=s_q[h * 64 : (h + 1) * 64, c0 : c0 + 512], in_=kstg[:, :]),
                reads=["l3kstg"], writes=[f"sq{h}"], dma=True)
            for tt in range(4):
                fns = [(lambda e, k=k, co=co, tt=tt: e.matmul(
                    pb[1][:, 0:64], lnc[k][:, tt * 128 : (tt + 1) * 128], wvT[k][:, co : co + 64],
                    start=(k == 0), stop=False)) for k in range(8)]
                fns.append(lambda e, co=co: e.matmul(
                    pb[1][:, 0:64], em.ones_row[0:1, 0:128],
                    brow[0:1, co : co + 64], start=False, stop=True))
                P.emit("te", fns, reads=["wvT", "l3brow", "ones_row"] + lnck, writes=[pb[1].name])
                P.emit("dve", lambda e: e.tensor_copy(vstg[:, 0:64], pb[1][:, 0:64]),
                       reads=[pb[1].name], writes=["l3vstg"])
                P.emit("sp", lambda e, h=h, c0=c0, tt=tt: e.dma_start(
                    out=s_v[h, c0 + tt * 128 : c0 + (tt + 1) * 128, 0:65], in_=vstg[:, 0:65]),
                    reads=["l3vstg"], writes=[f"sv{h}"], dma=True)

    s_projrev = nc.dram_tensor("s_projrev", [4096, 1280], F32, kind="Internal").ap()
    s_pdq = nc.dram_tensor("s_pdq", [512, 384], BF16, kind="Internal").ap()
    identf = P.sb("identf", [128, 128], F32)
    P.emit("sp", lambda e: e.dma_start(out=identf[:, :], in_=d_ident), writes=["identf"], dma=True)

    qt = P.sb("l3qt", [64, 4096], BF16)
    ktb = P.sb("l3ktb", [64, 4096], BF16)
    vb = [P.sb(f"l3vb{i}", [128, 65 * 8], BF16) for i in range(4)]
    projk = P.sb("l3projk", [128, 257], F32)
    prevt = P.sb("l3prevt", [128, 1280], F32)
    projL = P.sb("l3projL", [1, 4096], BF16)
    projR = P.sb("l3projR", [1, 4096], BF16)
    nbound = P.sb("l3nbv", [1, 4096], BF16)
    kmax2 = P.sb("l3kmax2", [1, 4], F32)
    probs = P.sb("l3probs", [128, 512], BF16)
    addend = [P.sb(f"l3addend{i}", [128, 128], F32) for i in range(4)]
    eq = P.sb("l3eq", [128, 256], BF16)
    p2t = [P.sb(f"l3p2t{i}", [128, 512], BF16) for i in range(2)]
    p2t2 = P.sb("l3p2t2", [1, 512], BF16)
    trow = P.sb("l3trow", [1, 512 * 3], F32)
    rrow = P.sb("l3rrow", [1, 512], F32)
    rbc = P.sb("l3rbc", [64, 512], F32)
    aoT = [P.sb(f"l3ao{i}", [128, 4096], BF16) for i in range(4)]
    aok = [f"l3ao{i}" for i in range(4)]
    mstg = P.sb("l3mstg", [128, 512], BF16)
    tstg = P.sb("l3tstg", [128, 128], BF16)
    zstg = P.sb("l3zstg", [128, 384], BF16)
    P.emit("dve", lambda e: e.memset(zstg[:, :], 0.0), writes=["l3zstg"])
    sq64 = em.scr[0]
    brw = em.scr[3]
    pb4 = P.ps("pb4", [64, 512], F32)
    wsb = em.scr[1]

    for h in range(8):
        P.emit("sp", lambda e, h=h: e.dma_start(out=ktb[:, :], in_=s_kt[h, :, :]),
               reads=[f"skt{h}"], writes=["l3ktb"], dma=True)
        P.emit("sp", lambda e, h=h: e.dma_start(out=qt[:, :], in_=s_q[h * 64 : (h + 1) * 64, :]),
               reads=[f"sq{h}"], writes=["l3qt"], dma=True)
        for vt in range(4):
            P.emit("sp", lambda e, h=h, vt=vt: e.dma_start(
                out=vb[vt][:, :],
                in_=AP(s_v.tensor, h * 4096 * 65 + vt * 8 * 128 * 65, [[65, 128], [128 * 65, 8], [1, 65]])),
                reads=[f"sv{h}"], writes=["l3vb"], dma=True)
        P.emit("dve", lambda e: e.memset(kmax2[0:1, 0:4], 0.0), writes=["l3kmax", "l3kmaxb", "l3kmaxc"])
        for c0 in range(0, 4096, 512):
            P.emit("act", lambda e, c0=c0: e.activation(sq64[0:64, :], ktb[:, c0 : c0 + 512], AF.Square),
                   reads=["l3ktb"], writes=[sq64.name])
            P.emit("te", lambda e: e.matmul(pb[3][0:1, 0:512], em.ones_col[0:64, :], sq64[0:64, :],
                   start=True, stop=True), reads=[sq64.name, "ones_col"], writes=[pb[3].name])
            P.emit("dve", lambda e: e.tensor_reduce(kmax2[0:1, 1:2], pb[3][0:1, 0:512], AX.X, ALU.max),
                   reads=[pb[3].name], writes=["l3kmaxb"])
            P.emit("dve", lambda e: e.tensor_tensor(kmax2[0:1, 0:1], kmax2[0:1, 0:1],
                   kmax2[0:1, 1:2], ALU.max), reads=["l3kmax", "l3kmaxb"], writes=["l3kmax"])
        P.emit("act", lambda e: e.activation(sq64[0:64, 0:257], relkT[:, :], AF.Square),
               reads=["relkT"], writes=[sq64.name])
        P.emit("te", lambda e: e.matmul(pb[3][0:1, 0:257], em.ones_col[0:64, :], sq64[0:64, 0:257],
               start=True, stop=True), reads=[sq64.name, "ones_col"], writes=[pb[3].name])
        P.emit("dve", lambda e: e.tensor_reduce(kmax2[0:1, 2:3], pb[3][0:1, 0:257], AX.X, ALU.max),
               reads=[pb[3].name], writes=["l3kmaxb"])
        P.emit("act", lambda e: e.activation(kmax2[0:1, 0:3], kmax2[0:1, 0:3], AF.Ln),
               reads=["l3kmax", "l3kmaxb"], writes=["l3kmax"])
        P.emit("act", lambda e: e.activation(kmax2[0:1, 0:3], kmax2[0:1, 0:3], AF.Exp, scale=0.5),
               reads=["l3kmax"], writes=["l3kmax"])
        P.emit("dve", lambda e: e.tensor_tensor(kmax2[0:1, 3:4], kmax2[0:1, 0:1], kmax2[0:1, 2:3],
               ALU.add), reads=["l3kmax"], writes=["l3kmaxc"])
        P.emit("dve", lambda e: e.tensor_scalar(kmax2[0:1, 3:4], kmax2[0:1, 3:4], 1.02, None, ALU.mult),
               reads=["l3kmaxc"], writes=["l3kmaxc"])
        for qt32 in range(32):
            q0 = qt32 * 128
            P.emit("act", lambda e, q0=q0: e.activation(sq64[0:64, 0:128], qt[:, q0 : q0 + 128],
                   AF.Square), reads=["l3qt"], writes=[sq64.name])
            P.emit("te", lambda e: e.matmul(pb[3][0:1, 0:128], em.ones_col[0:64, :],
                   sq64[0:64, 0:128], start=True, stop=True), reads=[sq64.name, "ones_col"],
                   writes=[pb[3].name])
            P.emit("dve", lambda e: e.tensor_scalar(brw[0:1, 0:128], pb[3][0:1, 0:128], 1e-30,
                   None, ALU.max), reads=[pb[3].name], writes=[brw.name])
            P.emit("act", lambda e: e.activation(brw[0:1, 0:128], brw[0:1, 0:128], AF.Ln),
                   reads=[brw.name], writes=[brw.name])
            P.emit("act", lambda e: e.activation(brw[0:1, 0:128], brw[0:1, 0:128], AF.Exp, scale=0.5),
                   reads=[brw.name], writes=[brw.name])
            P.emit("dve", lambda e, q0=q0: e.tensor_scalar(
                nbound[0:1, q0 : q0 + 128], brw[0:1, 0:128], kmax2[0:1, 3:4], -1.0, ALU.mult, ALU.mult),
                reads=[brw.name, "l3kmaxc"], writes=["l3nbound"])
            P.emit("te", lambda e, q0=q0: e.matmul(
                pb[0][:, 0:257], qt[:, q0 : q0 + 128], relkT[:, :], start=True, stop=True),
                reads=["l3qt", "relkT"], writes=[pb[0].name])
            P.emit("dve", lambda e: e.tensor_copy(projk[:, 0:257], pb[0][:, 0:257]),
                   reads=[pb[0].name], writes=["l3projk"])
            P.emit("dve", lambda e: e.tensor_copy(
                prevt[:, 0:511], AP(projk.ap().tensor, 256, [[257, 128], [0, 511]])),
                reads=["l3projk"], writes=["l3prevt"])
            P.emit("dve", lambda e: e.tensor_copy(
                prevt[:, 511:768], AP(projk.ap().tensor, 256, [[257, 128], [-1, 257]])),
                reads=["l3projk"], writes=["l3prevt"])
            P.emit("dve", lambda e: e.tensor_copy(
                prevt[:, 768:1280], AP(projk.ap().tensor, 0, [[257, 128], [0, 512]])),
                reads=["l3projk"], writes=["l3prevt"])
            P.emit("sp", lambda e, q0=q0: e.dma_start(
                out=s_projrev[q0 : q0 + 128, :], in_=prevt[:, :]),
                reads=["l3prevt"], writes=[f"sprev{qt32}"], dma=True)
            P.emit("te", [
                (lambda e, q0=q0: e.matmul(pb[3][0:1, 0:128], relkT2c[:, 0:1],
                                           qt[:, q0 : q0 + 128], start=True, stop=True)),
                (lambda e, q0=q0: e.matmul(pb[3][0:1, 128:256], relkT2c[:, 1:2],
                                           qt[:, q0 : q0 + 128], start=True, stop=True)),
            ], reads=["relkT2c", "l3qt"], writes=[pb[3].name])
            P.emit("dve", lambda e, q0=q0: e.tensor_copy(projR[0:1, q0 : q0 + 128], pb[3][0:1, 0:128]),
                   reads=[pb[3].name], writes=["l3projR"])
            P.emit("dve", lambda e, q0=q0: e.tensor_copy(projL[0:1, q0 : q0 + 128], pb[3][0:1, 128:256]),
                   reads=[pb[3].name], writes=["l3projL"])
        for qc in range(8):
            q0 = qc * 512
            pw = pb[0]
            diag_lo, diag_hi = 4 * qc - 1, 4 * qc + 4
            L_full = [kt for kt in range(32) if kt <= 4 * qc - 2]
            L_mask = {kt: kt - (4 * qc - 1) for kt in range(max(0, 4 * qc - 1), 4 * qc + 3)}
            contrib = L_full + sorted(L_mask)
            for qs in range(4):
                qtg = 4 * qc + qs
                for dk in (-1, 0, 1):
                    if not (0 <= qtg + dk <= 31):
                        cc = (1 - dk) * 128
                        P.emit("sp", lambda e, qs=qs, cc=cc: e.dma_start(
                            out=s_pdq[qs * 128 : (qs + 1) * 128, cc : cc + 128],
                            in_=zstg[:, 0:128]), reads=["l3zstg"], writes=[f"spdq{qs}_{cc}"], dma=True)
            for kt_i in range(32):
                ps = pb[1 + kt_i % 2]
                is_diag = diag_lo <= kt_i <= diag_hi
                fns = [
                    (lambda e, kt_i=kt_i, q0=q0, ps=ps: e.matmul(
                        ps[:, 0:512], ktb[:, kt_i * 128 : (kt_i + 1) * 128],
                        qt[:, q0 : q0 + 512], start=True, stop=False)),
                    (lambda e, q0=q0, ps=ps, st2=(not is_diag and not (kt_i <= 4 * qc - 2)
                        and not (kt_i >= 4 * qc + 5)): e.matmul(
                        ps[:, 0:512], em.ones_row[0:1, 0:128],
                        nbound[0:1, q0 : q0 + 512], start=False, stop=st2)),
                ]
                if kt_i <= 4 * qc - 2:
                    fns.append(lambda e, q0=q0, ps=ps: e.matmul(
                        ps[:, 0:512], em.ones_row[0:1, 0:128],
                        projL[0:1, q0 : q0 + 512], start=False, stop=True))
                elif kt_i >= 4 * qc + 5:
                    fns.append(lambda e, q0=q0, ps=ps: e.matmul(
                        ps[:, 0:512], em.ones_row[0:1, 0:128],
                        projR[0:1, q0 : q0 + 512], start=False, stop=True))
                rd = ["l3ktb", "l3qt", "ones_row", "l3nbound", "l3projL", "l3projR"]
                if is_diag:
                    for qs in range(4):
                        off = (q0 + qs * 128) * 1279 + 639 + kt_i * 128
                        P.emit("sp", lambda e, off=off, qs=qs: e.dma_start(
                            out=addend[qs][:, :],
                            in_=AP(s_projrev.tensor, off, [[1279, 128], [1, 128]])),
                            reads=[f"sprev{4 * qc + qs}"], writes=[f"l3addend{qs}"], dma=True)
                        fns.append(lambda e, qs=qs, ps=ps: e.matmul(
                            ps[:, qs * 128 : (qs + 1) * 128], addend[qs][:, :], identf[:, :],
                            start=False, stop=(qs == 3)))
                    rd += [f"l3addend{i}" for i in range(4)] + ["identf"]
                P.emit("te", fns, reads=rd, writes=[ps.name])
                P.emit("act", lambda e, ps=ps: e.activation(probs[:, :], ps[:, 0:512], AF.Exp),
                       reads=[ps.name], writes=["l3probs"])
                if kt_i in L_full:
                    P.emit("te", lambda e, fl=(kt_i == contrib[0]), ll=(kt_i == contrib[-1]): e.matmul(
                        pb[3][0:1, 0:512], em.ones_col_b[:, :], probs[:, :],
                        start=fl, stop=ll), reads=["l3probs", "ones_col"], writes=[pb[3].name])
                elif kt_i in L_mask:
                    mi = L_mask[kt_i]
                    P.emit("dve", lambda e, mi=mi: e.tensor_tensor(
                        mstg[:, :], probs[:, :], maskt[:, mi * 512 : (mi + 1) * 512], ALU.mult),
                        reads=["l3probs", "maskt"], writes=["l3mstg"])
                    P.emit("te", lambda e, fl=(kt_i == contrib[0]), ll=(kt_i == contrib[-1]): e.matmul(
                        pb[3][0:1, 0:512], em.ones_col_b[:, :], mstg[:, :],
                        start=fl, stop=ll), reads=["l3mstg", "ones_col"], writes=[pb[3].name])
                P.emit("te", lambda e, kt_i=kt_i, pw=pw: e.matmul(
                    pw[0:65, 0:512], vb[kt_i // 8][:, (kt_i % 8) * 65 : (kt_i % 8 + 1) * 65],
                    probs[:, :], start=(kt_i == 0), stop=(kt_i == 31)),
                    reads=["l3probs", "l3vb"], writes=[pw.name])
                if is_diag:
                    for qs in range(4):
                        qtg = 4 * qc + qs
                        dk = kt_i - qtg
                        if -1 <= dk <= 1:
                            P.emit("te", lambda e, qs=qs: e.transpose(
                                em.ptr[0][:, 0:128], probs[:, qs * 128 : (qs + 1) * 128],
                                em.ident[:, :]), reads=["l3probs", "ident"], writes=["ptr0"])
                            P.emit("dve", lambda e: e.tensor_copy(
                                AP(tstg.ap().tensor, 127, [[128, 128], [-1, 128]]),
                                em.ptr[0][:, 0:128]), reads=["ptr0"], writes=["l3tstg"])
                            cc = (1 - dk) * 128
                            P.emit("sp", lambda e, qs=qs, cc=cc: e.dma_start(
                                out=s_pdq[qs * 128 : (qs + 1) * 128, cc : cc + 128],
                                in_=tstg[:, :]), reads=["l3tstg"], writes=[f"spdq{qs}_{cc}"], dma=True)
            P.emit("dve", lambda e: e.tensor_copy(trow[0:1, 0:512], pb[3][0:1, 0:512]),
                   reads=[pb[3].name], writes=["l3trow"])
            for qs in range(4):
                off = (qs * 128) * 384 + 127
                P.emit("sp", lambda e, off=off: e.dma_start(
                    out=eq[:, 0:256], in_=AP(s_pdq.tensor, off, [[383, 128], [1, 256]])),
                    reads=[f"spdq{qs}_{cc}" for cc in (0, 128, 256)], writes=["l3eq"], dma=True)
                for mt in range(2):
                    P.emit("te", lambda e, mt=mt: e.transpose(
                        em.ptr[1][:, 0:128], eq[:, mt * 128 : (mt + 1) * 128], em.ident[:, :]),
                        reads=["l3eq", "ident"], writes=["ptr1"])
                    P.emit("dve", lambda e, mt=mt, qs=qs: e.tensor_copy(
                        p2t[mt][:, qs * 128 : (qs + 1) * 128], em.ptr[1][:, 0:128]),
                        reads=["ptr1"], writes=[f"l3p2t{mt}"])
            P.emit("dve", lambda e: e.memset(p2t[0][0:1, :], 0.0), writes=["l3p2t0"])
            P.emit("te", [
                (lambda e: e.matmul(pb[3][0:1, 0:512], em.ones_col_b[:, :], p2t[0][:, :],
                                    start=True, stop=False)),
                (lambda e: e.matmul(pb[3][0:1, 0:512], em.ones_col_b[:, :], p2t[1][:, :],
                                    start=False, stop=True)),
            ], reads=["l3p2t0", "l3p2t1", "ones_col"], writes=[pb[3].name])
            P.emit("dve", lambda e: e.tensor_copy(trow[0:1, 512:1024], pb[3][0:1, 0:512]),
                   reads=[pb[3].name], writes=["l3trowb"])
            P.emit("dve", lambda e, pw=pw: e.tensor_tensor(
                trow[0:1, 1024:1536], pw[64:65, 0:512], trow[0:1, 512:1024], ALU.subtract),
                reads=[pw.name, "l3trowb"], writes=["l3trow2"])
            P.emit("dve", lambda e: e.tensor_tensor(
                trow[0:1, 1024:1536], trow[0:1, 1024:1536], trow[0:1, 0:512], ALU.subtract),
                reads=["l3trow2", "l3trow"], writes=["l3trow2"])
            P.emit("dve", lambda e: e.tensor_copy(p2t[0][0:1, :], trow[0:1, 1024:1536]),
                   reads=["l3trow2"], writes=["l3p2t0"])
            P.emit("dve", lambda e: e.tensor_copy(p2t2[0:1, :], trow[0:1, 0:512]),
                   reads=["l3trow"], writes=["l3p2t2"])
            P.emit("te", [
                (lambda e: e.matmul(pb4[0:64, 0:512], relv[0][:, 0:64], p2t[0][:, :],
                                    start=True, stop=False)),
                (lambda e: e.matmul(pb4[0:64, 0:512], relv[1][:, 0:64], p2t[1][:, :],
                                    start=False, stop=False)),
                (lambda e: e.matmul(pb4[0:64, 0:512], relv2[0:1, 0:64], p2t2[0:1, :],
                                    start=False, stop=True)),
            ], reads=["relv", "l3p2t0", "l3p2t1", "l3p2t2"], writes=["pb4"])
            P.emit("dve", lambda e, pw=pw: e.reciprocal(rrow[0:1, :], pw[64:65, 0:512]),
                   reads=[pw.name], writes=["l3rrow"])
            P.emit("te", lambda e: e.matmul(pb[3][0:64, 0:512], em.ones_row_f[0:1, 0:64],
                   rrow[0:1, :], start=True, stop=True), reads=["l3rrow", "ones_row"], writes=[pb[3].name])
            P.emit("dve", lambda e: e.tensor_copy(rbc[:, :], pb[3][0:64, 0:512]),
                   reads=[pb[3].name], writes=["l3rbc"])
            P.emit("dve", lambda e, pw=pw: e.tensor_copy(wsb[0:64, :], pw[0:64, 0:512]),
                   reads=[pw.name], writes=[wsb.name])
            P.emit("dve", lambda e: e.tensor_tensor(wsb[0:64, :], wsb[0:64, :], pb4[0:64, 0:512],
                   ALU.add), reads=[wsb.name, "pb4"], writes=[wsb.name])
            P.emit("dve", lambda e, h=h, q0=q0: e.tensor_tensor(
                aoT[h // 2][(h % 2) * 64 : (h % 2) * 64 + 64, q0 : q0 + 512],
                wsb[0:64, :], rbc[:, :], ALU.mult),
                reads=[wsb.name, "l3rbc"], writes=[aok[h // 2]])

    ostg = P.sb("l3ostg", [128, 512], F32)
    for co in range(8):
        for c0 in range(0, 4096, 512):
            pbank = pb[co % 2]
            P.emit("te", [(lambda e, k=k, co=co, c0=c0, pbank=pbank: e.matmul(
                pbank[:, 0:512], woT[k][:, co * 128 : (co + 1) * 128], aoT[k][:, c0 : c0 + 512],
                start=(k == 0), stop=(k == 3))) for k in range(4)],
                reads=["woT"] + aok, writes=[pbank.name])
            P.emit("dve", lambda e, pbank=pbank: e.tensor_copy(ostg[:, :], pbank[:, 0:512]),
                   reads=[pbank.name], writes=["l3ostg"])
            P.emit("sp", lambda e, co=co, c0=c0: e.dma_start(
                out=o_part[co * 128 : (co + 1) * 128, c0 : c0 + 512], in_=ostg[:, :]),
                reads=["l3ostg"], writes=[], dma=True)

    P.finalize()
    return nc, P





def build_L3b():
    nc = bass.Bass("TRN2", target_bir_lowering=False)
    d_x = nc.dram_tensor("x", [1024, 2048], F32, kind="ExternalInput").ap()
    d_p0 = nc.dram_tensor("p0", [1024, 2048], F32, kind="ExternalInput").ap()
    d_p1 = nc.dram_tensor("p1", [1024, 2048], F32, kind="ExternalInput").ap()
    d_bo = nc.dram_tensor("bo", [1, 1024], F32, kind="ExternalInput").ap()
    d_rw1 = nc.dram_tensor("rw1", [1024, 1024], F32, kind="ExternalInput").ap()
    d_rb1 = nc.dram_tensor("rb1", [1, 1024], F32, kind="ExternalInput").ap()
    d_rw2 = nc.dram_tensor("rw2", [1024, 8], F32, kind="ExternalInput").ap()
    d_rb2 = nc.dram_tensor("rb2", [1, 8], F32, kind="ExternalInput").ap()
    d_ew1 = nc.dram_tensor("ew1", [8, 1024, 2048], F32, kind="ExternalInput").ap()
    d_eb1 = nc.dram_tensor("eb1", [8, 2048], F32, kind="ExternalInput").ap()
    d_ew2 = nc.dram_tensor("ew2", [8, 2048, 1024], F32, kind="ExternalInput").ap()
    d_eb2 = nc.dram_tensor("eb2", [8, 1024], F32, kind="ExternalInput").ap()
    d_ident = nc.dram_tensor("ident", [128, 128], F32, kind="ExternalInput").ap()
    d_xo = nc.dram_tensor("xo", [1024, 2048], F32, kind="ExternalOutput").ap()

    P = Prog(nc)
    em = Em(P)
    em.wstg_width = 128
    em.no_gb01 = True
    em.setup_consts(d_ident, None)
    pb = em.pb
    gbw = P.sb("gbw", [2, 1024], F32)
    P.emit("dve", lambda e: e.memset(gbw[0:2, :], 0.0), writes=["gbw"])
    P.emit("dve", lambda e: e.memset(gbw[0:1, :], 1.0), writes=["gbw"])
    ones128f = P.sb("ones128f", [1, 128], F32)
    P.emit("dve", lambda e: e.memset(ones128f[:, :], 1.0), writes=["ones128f"])
    identf = P.sb("identf", [128, 128], F32)
    P.emit("pool", lambda e: e.dma_start(out=identf[:, :], in_=d_ident), writes=["identf"], dma=True)
    bo_pc, _ = em.load_pcol(d_bo.rearrange("a b -> (a b)"), 1024, key="bo_pc")
    rb1_pc, _ = em.load_pcol(d_rb1.rearrange("a b -> (a b)"), 1024, key="rb1_pc")
    eb1_pc = P.sb("eb1_pc", [128, 128], F32)
    P.emit("pool", [(lambda e, ee=ee: e.dma_start(
        out=eb1_pc[:, ee * 16 : (ee + 1) * 16],
        in_=AP(d_eb1.tensor, ee * 2048, [[1, 128], [128, 16]]), allow_slow_non_contiguous=True))
        for ee in range(8)], writes=["eb1_pc"], dma=True)
    eb2_pc = P.sb("eb2_pc", [128, 64], F32)
    P.emit("pool", [(lambda e, ee=ee: e.dma_start(
        out=eb2_pc[:, ee * 8 : (ee + 1) * 8],
        in_=AP(d_eb2.tensor, ee * 1024, [[1, 128], [128, 8]]), allow_slow_non_contiguous=True))
        for ee in range(8)], writes=["eb2_pc"], dma=True)

    x1 = [P.sb(f"b_x1{i}", [128, 2048], F32) for i in range(8)]
    x1k = [f"b_x1{i}" for i in range(8)]
    stg = [P.sb(f"b_stg{i}", [128, 512], F32) for i in range(3)]
    for ct in range(8):
        for c0 in range(0, 2048, 512):
            P.emit("sp", lambda e, ct=ct, c0=c0: e.dma_start(
                out=stg[0][:, :], in_=d_x[ct * 128 : (ct + 1) * 128, c0 : c0 + 512]),
                writes=[stg[0].name], dma=True)
            P.emit("sp", lambda e, ct=ct, c0=c0: e.dma_start(
                out=stg[1][:, :], in_=d_p0[ct * 128 : (ct + 1) * 128, c0 : c0 + 512]),
                writes=[stg[1].name], dma=True)
            P.emit("sp", lambda e, ct=ct, c0=c0: e.dma_start(
                out=stg[2][:, :], in_=d_p1[ct * 128 : (ct + 1) * 128, c0 : c0 + 512]),
                writes=[stg[2].name], dma=True)
            P.emit("dve", lambda e: e.tensor_tensor(stg[1][:, :], stg[1][:, :], stg[2][:, :],
                   ALU.add), reads=[stg[1].name, stg[2].name], writes=[stg[1].name])
            P.emit("dve", lambda e, ct=ct, c0=c0: e.scalar_tensor_tensor(
                x1[ct][:, c0 : c0 + 512], stg[1][:, :], bo_pc[:, ct : ct + 1], stg[0][:, :],
                ALU.add, ALU.add), reads=[stg[1].name, "bo_pc", stg[0].name], writes=[x1k[ct]])

    tln = [P.sb(f"b_tln{i}", [128, 2048], BF16) for i in range(8)]
    tlnk = [f"b_tln{i}" for i in range(8)]
    r1T = [P.sb(f"b_r1T{i}", [128, 2048], BF16) for i in range(8)]
    r1Tk = [f"b_r1T{i}" for i in range(8)]
    ew1t = [P.sb(f"b_ew1{i}", [128, 2048], BF16) for i in range(8)]
    ehc = [P.sb(f"b_ehc{i}", [128, 512], BF16) for i in range(16)]
    ehck = [f"b_ehc{i}" for i in range(16)]
    rw1h = [[ew1t[ot // 2][:, ((ot % 2) * 8 + k) * 128 : ((ot % 2) * 8 + k + 1) * 128]
             for ot in range(8)] for k in range(8)]
    rw1l = [[ew1t[4 + ot // 2][:, ((ot % 2) * 8 + k) * 128 : ((ot % 2) * 8 + k + 1) * 128]
             for ot in range(8)] for k in range(8)]
    fstg = [stg[1 + i] for i in range(2)]
    for ot in range(8):
        P.emit("pool", [(lambda e, k=k, ot=ot: e.dma_start(
            out=rw1h[k][ot],
            in_=d_rw1[k * 128 : (k + 1) * 128, ot * 128 : (ot + 1) * 128])) for k in range(8)],
            writes=["b_ew1"], dma=True)
        for k in range(8):
            fs = fstg[k % 2]
            P.emit("sp", lambda e, k=k, ot=ot, fs=fs: e.dma_start(
                out=fs[:, 0:128], in_=d_rw1[k * 128 : (k + 1) * 128, ot * 128 : (ot + 1) * 128]),
                writes=[fs.name], dma=True)
            P.emit("dve", lambda e, k=k, ot=ot, fs=fs: e.tensor_tensor(
                rw1l[k][ot], fs[:, 0:128], rw1h[k][ot], ALU.subtract),
                reads=[fs.name, "b_ew1"], writes=["b_ew1"])
    rw2f = [P.sb(f"b_rw2f{i}", [128, 8], F32) for i in range(8)]
    P.emit("pool", [(lambda e, k=k: e.dma_start(
        out=rw2f[k][:, :], in_=d_rw2[k * 128 : (k + 1) * 128, :])) for k in range(8)],
        writes=["b_rw2"], dma=True)
    rb2rep = P.sb("b_rb2rep", [1, 128], F32)
    P.emit("pool", [(lambda e, tt=tt: e.dma_start(
        out=rb2rep[0:1, tt * 8 : (tt + 1) * 8], in_=d_rb2)) for tt in range(16)],
        writes=["b_rb2rep"], dma=True)
    r1f = P.sb("b_r1f", [128, 512], F32)
    lgall = P.sb("b_lgall", [128, 128], F32)
    u2all = P.sb("b_u2all", [128, 128], F32)
    lcol = P.sb("b_lcol", [128, 64], F32)
    d_wr = nc.dram_tensor("s_wr", [8, 2048], F32, kind="Internal").ap()
    wstage = P.sb("b_wstage", [8, 128], F32)
    P.emit("dve", lambda e: e.memset(lgall[:, :], 0.0), writes=["b_lgall"])
    tl8 = [ehc[k] for k in range(8)]
    tlk8 = [ehck[k] for k in range(8)]
    for cg in range(4):
        c0g = cg * 512
        em.ln(x1, x1k, 512, tln, tlnk, gb=gbw, gbkey="gbw", Toff=c0g, out_off=c0g,
              lo_out=tl8, lo_keys=tlk8)
        for ot in range(8):
            par = ot % 2
            pbank = pb[par]
            fns = []
            for k in range(8):
                fns.append(lambda e, k=k, ot=ot, c0g=c0g, pbank=pbank: e.matmul(
                    pbank[:, 0:512], rw1h[k][ot], tln[k][:, c0g : c0g + 512],
                    start=(k == 0), stop=False))
            for k in range(8):
                fns.append(lambda e, k=k, ot=ot, pbank=pbank: e.matmul(
                    pbank[:, 0:512], rw1h[k][ot], ehc[k][:, :], start=False, stop=False))
            for k in range(8):
                fns.append(lambda e, k=k, ot=ot, c0g=c0g, pbank=pbank: e.matmul(
                    pbank[:, 0:512], rw1l[k][ot], tln[k][:, c0g : c0g + 512],
                    start=False, stop=(k == 7)))
            P.emit("te", fns, reads=["b_ew1"] + tlnk + tlk8, writes=[pbank.name])
            P.emit("act", lambda e, ot=ot, pbank=pbank: e.activation(
                r1f[:, :], pbank[:, 0:512], AF.Silu,
                bias=rb1_pc[:, ot : ot + 1]), reads=[pbank.name, "rb1_pc"], writes=["b_r1f"])
            P.emit("te", [(lambda e, tl=tl, ot=ot, cg=cg: e.matmul(
                pb[2][:, (4 * cg + tl) * 8 : (4 * cg + tl) * 8 + 8],
                r1f[:, tl * 128 : (tl + 1) * 128], rw2f[ot][:, :],
                start=True, stop=True)) for tl in range(4)],
                reads=["b_r1f", "b_rw2"], writes=[pb[2].name])
            P.emit("dve", lambda e, cg=cg: e.tensor_tensor(
                lgall[:, 32 * cg : 32 * cg + 32], lgall[:, 32 * cg : 32 * cg + 32],
                pb[2][:, 32 * cg : 32 * cg + 32], ALU.add),
                reads=["b_lgall", pb[2].name], writes=["b_lgall"])
    P.emit("te", lambda e: e.matmul(pb[2][:, 0:128], ones128f[:, :], rb2rep[0:1, :],
           start=True, stop=True), reads=["ones128f", "b_rb2rep"], writes=[pb[2].name])
    P.emit("dve", lambda e: e.tensor_tensor(lgall[:, :], lgall[:, :], pb[2][:, 0:128], ALU.add),
           reads=["b_lgall", pb[2].name], writes=["b_lgall"])

    for tt in range(16):
        lg = lgall[:, tt * 8 : (tt + 1) * 8]
        u2 = u2all[:, tt * 8 : (tt + 1) * 8]
        c = lcol[:, tt * 4 : (tt + 1) * 4]
        P.emit("dve", lambda e, lg=lg, c=c: e.tensor_reduce(c[:, 0:1], lg, AX.X, ALU.max),
               reads=["b_lgall"], writes=["b_lcol"])
        P.emit("dve", lambda e, c=c: e.tensor_scalar(c[:, 1:2], c[:, 0:1], -1.0, None, ALU.mult),
               reads=["b_lcol"], writes=["b_lcol"])
        P.emit("dve", lambda e, lg=lg, u2=u2, c=c: e.tensor_scalar(u2, lg, c[:, 0:1], None,
               ALU.is_ge), reads=["b_lgall", "b_lcol"], writes=["b_u2all"])
        P.emit("dve", lambda e, lg=lg, u2=u2: e.scalar_tensor_tensor(u2, u2, -1e30, lg,
               ALU.mult, ALU.add), reads=["b_u2all", "b_lgall"], writes=["b_u2all"])
        P.emit("dve", lambda e, u2=u2, c=c: e.tensor_reduce(c[:, 2:3], u2, AX.X, ALU.max),
               reads=["b_u2all"], writes=["b_lcol"])
        P.emit("dve", lambda e, lg=lg, u2=u2, c=c: e.tensor_scalar(u2, lg, c[:, 2:3], None,
               ALU.is_ge), reads=["b_lgall", "b_lcol"], writes=["b_u2all"])
    for tt in range(16):
        P.emit("act", lambda e, tt=tt: e.activation(
            lgall[:, tt * 8 : (tt + 1) * 8], lgall[:, tt * 8 : (tt + 1) * 8], AF.Exp,
            bias=lcol[:, tt * 4 + 1 : tt * 4 + 2]), reads=["b_lgall", "b_lcol"],
            writes=["b_lgall"])
    for tt in range(16):
        lg = lgall[:, tt * 8 : (tt + 1) * 8]
        u2 = u2all[:, tt * 8 : (tt + 1) * 8]
        c = lcol[:, tt * 4 : (tt + 1) * 4]
        P.emit("dve", lambda e, lg=lg, u2=u2: e.tensor_tensor(u2, u2, lg, ALU.mult),
               reads=["b_u2all", "b_lgall"], writes=["b_u2all"])
        P.emit("dve", lambda e, u2=u2, c=c: e.tensor_reduce(c[:, 3:4], u2, AX.X, ALU.add),
               reads=["b_u2all"], writes=["b_lcol"])
        P.emit("dve", lambda e, c=c: e.reciprocal(c[:, 3:4], c[:, 3:4]),
               reads=["b_lcol"], writes=["b_lcol"])
        P.emit("dve", lambda e, u2=u2, c=c: e.tensor_scalar(u2, u2, c[:, 3:4], None, ALU.mult),
               reads=["b_u2all", "b_lcol"], writes=["b_u2all"])
        P.emit("te", lambda e, u2=u2: e.transpose(pb[3][0:8, 0:128], u2, identf[:, :]),
               reads=["b_u2all", "identf"], writes=[pb[3].name])
        P.emit("dve", lambda e: e.tensor_copy(wstage[0:8, 0:128],
               pb[3][0:8, 0:128]), reads=[pb[3].name], writes=["b_wstage"])
        P.emit("sp", lambda e, tt=tt: e.dma_start(out=d_wr[0:8, tt * 128 : (tt + 1) * 128],
               in_=wstage[0:8, 0:128]), reads=["b_wstage"], writes=["s_wr"], dma=True)

    for ct in range(8):
        P.emit("dve", lambda e, ct=ct: e.tensor_tensor(x1[ct][:, :], x1[ct][:, :], tln[ct][:, :],
               ALU.add), reads=[x1k[ct], tlnk[ct]], writes=[x1k[ct]])

    ew2t = [r1T[k // 2][:, (k % 2) * 1024 : (k % 2 + 1) * 1024] for k in range(16)]
    ew2k = [f"b_r1T{k // 2}" for k in range(16)]
    wre = r1f
    for ee in range(8):
        P.emit("pool", [(lambda e, k=k, ee=ee: e.dma_start(
            out=ew1t[k][:, :], in_=d_ew1[ee, k * 128 : (k + 1) * 128, :])) for k in range(8)],
            writes=["b_ew1"], dma=True)
        P.emit("pool", [(lambda e, k=k, ee=ee: e.dma_start(
            out=ew2t[k], in_=d_ew2[ee, k * 128 : (k + 1) * 128, :])) for k in range(16)],
            writes=r1Tk, dma=True)
        for c0 in range(0, 2048, 512):
            j = c0 // 512
            P.emit("sp", lambda e, ee=ee, c0=c0: e.dma_start(out=wre[0:1, 0:512],
                   in_=d_wr[ee : ee + 1, c0 : c0 + 512]),
                   reads=["s_wr"], writes=["b_r1f"], dma=True)
            P.emit("te", lambda e: e.matmul(pb[3][:, 0:512], ones128f[:, :],
                   wre[0:1, 0:512], start=True, stop=True),
                   reads=["ones128f", "b_r1f"], writes=[pb[3].name])
            P.emit("dve", lambda e, j=j: e.tensor_copy(em.scr[j][:, :], pb[3][:, 0:512]),
                   reads=[pb[3].name], writes=[f"c_scr{j}"])
        for c0 in range(0, 2048, 512):
            j = c0 // 512
            for ot in range(16):
                pbank = pb[ot % 2]
                P.emit("te", [(lambda e, k=k, ot=ot, c0=c0, pbank=pbank: e.matmul(
                    pbank[:, 0:512], ew1t[k][:, ot * 128 : (ot + 1) * 128],
                    tln[k][:, c0 : c0 + 512], start=(k == 0), stop=(k == 7))) for k in range(8)],
                    reads=["b_ew1"] + tlnk, writes=[pbank.name])
                P.emit("act", lambda e, ot=ot, ee=ee, pbank=pbank: e.activation(
                    ehc[ot][:, :], pbank[:, 0:512], AF.Silu,
                    bias=eb1_pc[:, ee * 16 + ot : ee * 16 + ot + 1]),
                    reads=[pbank.name, "eb1_pc"], writes=[ehck[ot]])
            for ot in range(8):
                P.emit("te", [(lambda e, k=k, ot=ot: e.matmul(
                    pb[2][:, 0:512], ew2t[k][:, ot * 128 : (ot + 1) * 128], ehc[k][:, :],
                    start=(k == 0), stop=(k == 15))) for k in range(16)],
                    reads=list(set(ew2k)) + ehck, writes=[pb[2].name])
                P.emit("dve", lambda e, ot=ot, ee=ee, j=j: e.scalar_tensor_tensor(
                    stg[0][:, :], pb[2][:, 0:512], eb2_pc[:, ee * 8 + ot : ee * 8 + ot + 1],
                    em.scr[j][:, :], ALU.add, ALU.mult),
                    reads=[pb[2].name, "eb2_pc", f"c_scr{j}"], writes=[stg[0].name])
                P.emit("dve", lambda e, ot=ot, c0=c0: e.tensor_tensor(
                    x1[ot][:, c0 : c0 + 512], x1[ot][:, c0 : c0 + 512], stg[0][:, :], ALU.add),
                    reads=[x1k[ot], stg[0].name], writes=[x1k[ot]])

    for ct in range(8):
        P.emit("sp", lambda e, ct=ct: e.dma_start(
            out=d_xo[ct * 128 : (ct + 1) * 128, :], in_=x1[ct][:, :]), reads=[x1k[ct]],
            writes=[], dma=True)

    P.finalize()
    return nc, P


def rel_masks():
    m = np.zeros((512, 512), np.float32)
    for mi in range(4):
        i = np.arange(128)[:, None]
        j = np.arange(512)[None, :]
        m[mi * 128 : (mi + 1) * 128] = (i - j <= -mi * 128).astype(np.float32)
    return m


def host_L3(x_fm, W, n_cores=8):
    ident = np.eye(128, dtype=np.float32)
    maskL = rel_masks()
    in_maps = []
    for core in range(n_cores):
        b, hg = core // 2, core % 2
        hs = slice(hg * 512, (hg + 1) * 512)
        in_maps.append(dict(
            xf=np.ascontiguousarray(x_fm[b]),
            wq=np.ascontiguousarray(W["L3_attn_Wq"][hs]), wk=np.ascontiguousarray(W["L3_attn_Wk"][hs]),
            wv=np.ascontiguousarray(W["L3_attn_Wv"][hs]), wo=np.ascontiguousarray(W["L3_attn_Wo"][:, hs]),
            bq=np.ascontiguousarray(W["L3_attn_bq"][hs].reshape(1, -1)),
            bk=np.ascontiguousarray(W["L3_attn_bk"][hs].reshape(1, -1)),
            bv=np.ascontiguousarray(W["L3_attn_bv"][hs].reshape(1, -1)),
            rel_k=W["L3_attn_rel_k_emb"], rel_v=W["L3_attn_rel_v_emb"],
            maskL=maskL, ident=ident))
    nc, P = build_L3a()
    res = run_bass_kernel_spmd(nc, in_maps, core_ids=list(range(n_cores)))
    P.close()
    parts = [res.results[c]["o_part"] for c in range(n_cores)]

    in_maps = []
    for core in range(n_cores):
        b, half = core // 2, core % 2
        tsl = slice(half * 2048, (half + 1) * 2048)
        in_maps.append(dict(
            x=np.ascontiguousarray(x_fm[b][:, tsl]),
            p0=np.ascontiguousarray(parts[2 * b][:, tsl]),
            p1=np.ascontiguousarray(parts[2 * b + 1][:, tsl]),
            bo=W["L3_attn_bo"].reshape(1, -1),
            rw1=W["L3_ffn_rw1"], rb1=W["L3_ffn_rb1"].reshape(1, -1),
            rw2=W["L3_ffn_rw2"], rb2=W["L3_ffn_rb2"].reshape(1, -1),
            ew1=W["L3_ffn_ew1"], eb1=W["L3_ffn_eb1"], ew2=W["L3_ffn_ew2"], eb2=W["L3_ffn_eb2"],
            ident=ident))
    nc, P = build_L3b()
    res = run_bass_kernel_spmd(nc, in_maps, core_ids=list(range(n_cores)))
    P.close()
    out = np.empty_like(x_fm)
    for core in range(n_cores):
        b, half = core // 2, core % 2
        out[b][:, half * 2048 : (half + 1) * 2048] = res.results[core]["xo"]
    return out


def host_L0(x_fm, W, n_cores=8):
    ident = np.eye(128, dtype=np.float32)
    gb01 = np.stack([np.ones(1024), np.zeros(1024)]).astype(np.float32)
    in_maps = []
    for core in range(n_cores):
        b, half = core // 2, core % 2
        lo = half * 2048 - 128
        hi = half * 2048 + 2048 + 128
        xh = np.zeros((1024, TH), np.float32)
        pen = np.zeros((1, TH), np.float32)
        s, e = max(lo, 0), min(hi, 4096)
        xh[:, s - lo : s - lo + (e - s)] = x_fm[b][:, s:e]
        if lo < 0:
            pen[0, 0:128] = 1.0
        if hi > 4096:
            pen[0, TH - 128 :] = 1.0
        in_maps.append(dict(
            x=np.ascontiguousarray(xh), pen=pen, ident=ident, gb01=gb01,
            ln1=np.stack([W["L0_attn_ln1_w"], W["L0_attn_ln1_b"]]).astype(np.float32),
            ln2=np.stack([W["L0_attn_ln2_w"], W["L0_attn_ln2_b"]]).astype(np.float32),
            qkv_w=W["L0_attn_qkv_w"], qkv_b=W["L0_attn_qkv_b"].reshape(1, -1),
            proj_w=W["L0_attn_proj_w"], proj_b=W["L0_attn_proj_b"].reshape(1, -1),
            fc1_w=W["L0_attn_fc1_w"], fc1_b=W["L0_attn_fc1_b"].reshape(1, -1),
            fc2_w=W["L0_attn_fc2_w"], fc2_b=W["L0_attn_fc2_b"].reshape(1, -1),
            ffn_w=W["L0_ffn_weight"], ffn_b=W["L0_ffn_bias"].reshape(1, -1),
        ))
    nc, P = build_L0()
    res = run_bass_kernel_spmd(nc, in_maps, core_ids=list(range(n_cores)))
    P.close()
    out = np.empty_like(x_fm)
    for core in range(n_cores):
        b, half = core // 2, core % 2
        out[b][:, half * 2048 : (half + 1) * 2048] = res.results[core]["xo"]
    return out




def kernel(**inputs):
    x = inputs["L0_attn_x"]
    x_fm = np.ascontiguousarray(np.asarray(x, np.float32).transpose(0, 2, 1))
    x_fm = host_L0(x_fm, inputs)
    x_fm = host_L1(x_fm, inputs)
    x_fm = host_L2(x_fm, inputs)
    x_fm = host_L2f(x_fm, inputs)
    x_fm = host_L3(x_fm, inputs)
    return np.ascontiguousarray(x_fm.transpose(0, 2, 1))
